# version 01
# baseline (speedup 1.0000x reference)
import numpy as np
import ml_dtypes
from contextlib import ExitStack
import concourse.bass as bass
import concourse.mybir as mybir
from concourse.bass_utils import run_bass_kernel_spmd

F32 = mybir.dt.float32
BF16 = mybir.dt.bfloat16
AF = mybir.ActivationFunctionType
ALU = mybir.AluOpType
AXX = mybir.AxisListType.X

B, L, D, HD, H, FF = 4, 4096, 2048, 256, 8, 8192
NT = 16
NDT = 16
NKT = 32
TOK = 2048
EPS_LN = 1e-5
EPS_RMS = 1e-6
VP = 272


class Sem:
    def __init__(self, ctx, nc, name):
        self.h = ctx.enter_context(nc.semaphore(name))
        self.n = 0

    def inc(self, instr, k=1):
        instr.then_inc(self.h, k)
        self.n += k
        return instr

    def wait(self, eng, val=None):
        eng.wait_ge(self.h, self.n if val is None else val)


class _PE:

    def __init__(self, t):
        self._t = t
        self.last = None

    def __getattr__(self, name):
        return getattr(self._t, name)

    def matmul(self, *a, **k):
        self.last = self._t.matmul(*a, **k)
        return self.last

    def transpose(self, *a, **k):
        self.last = self._t.transpose(*a, **k)
        return self.last


class MK:
    def __init__(self, phases=9, taps=()):
        self.phases = phases
        self.taps = taps
        self.tap_out = {}
        nc = self.nc = bass.Bass("TRN2", target_bir_lowering=False,
                                 detect_race_conditions=False)
        ap = lambda name, shape, dt=F32: nc.dram_tensor(name, list(shape), dt, kind="ExternalInput").ap()
        self.xT_full = ap("xT_full", [D, L])
        self.xT_own = ap("xT_own", [D, TOK])
        self.wqT = ap("wqT", [D, D])
        self.wkT = ap("wkT", [D, HD])
        self.wvT = ap("wvT", [D, HD])
        self.woT = ap("woT", [D, D])
        self.wgT = ap("wgT", [D, FF])
        self.wuT = ap("wuT", [D, FF])
        self.wdT = ap("wdT", [FF, D])
        self.qkn = ap("qkn", [128, 4])
        self.cosT = ap("cosT", [128, L])
        self.sinT = ap("sinT", [128, L])
        self.cos_own = ap("cos_own", [128, TOK])
        self.sin_own = ap("sin_own", [128, TOK])
        self.masks = ap("masks", [NT, 2, 128, 1024])
        self.id_f32 = ap("id_f32", [128, 128])
        self.outT = nc.dram_tensor("outT", [D, TOK], F32, kind="ExternalOutput").ap()
        self.ctxT_sc = nc.dram_tensor("ctxT_sc", [D, TOK], BF16, kind="Internal").ap()
        self.a_sc = nc.dram_tensor("a_sc", [D, TOK], F32, kind="Internal").ap()
        self.x1_sc = nc.dram_tensor("x1_sc", [D, TOK], F32, kind="Internal").ap()
        self.gated_sc = nc.dram_tensor("gated_sc", [FF, TOK], BF16, kind="Internal").ap()
        self.down_sc = nc.dram_tensor("down_sc", [D, TOK], F32, kind="Internal").ap()

    def tap(self, name, shape, dt=F32):
        if name in self.taps:
            t = self.nc.dram_tensor("tap_" + name, list(shape), dt, kind="ExternalOutput").ap()
            self.tap_out[name] = t
            return t
        return None

    def build(self):
        nc = self.nc
        ctx = ExitStack()
        with ctx:
            sb = lambda name, shape, dt: ctx.enter_context(nc.sbuf_tensor(name, list(shape), dt))
            self.idb = sb("idb", [128, 128], BF16)
            self.idf = sb("idf", [128, 128], F32)
            self.qkn_sb = sb("qkn_sb", [128, 4], F32)
            self.ones_col = sb("ones_col", [128, 1], BF16)
            self.ones_row = sb("ones_row", [1, 128], F32)
            self.rows = sb("rows", [1, 4096], F32)
            self.mx_cols = sb("mx_cols", [128, 8], F32)
            self.mj_cols = sb("mj_cols", [128, 8], F32)
            self.mrows_sb = sb("mrows_sb", [1, 1024], F32)
            self.consts = sb("consts", [128, 4], F32)
            nc.const_aps.register(F32, EPS_LN, self.consts[:, 0:1])
            nc.const_aps.register(F32, EPS_RMS, self.consts[:, 1:2])
            nc.const_aps.register(F32, -0.5, self.consts[:, 2:3])
            self.s_sy = Sem(ctx, nc, "s_sy")
            self.s_gp = Sem(ctx, nc, "s_gp")
            self.s_pe = Sem(ctx, nc, "s_pe")
            self.s_dve = Sem(ctx, nc, "s_dve")
            self.s_act = Sem(ctx, nc, "s_act")
            self.s_cst = Sem(ctx, nc, "s_cst")
            self.s_tab = Sem(ctx, nc, "s_tab")
            self.s_ld = [Sem(ctx, nc, f"s_ld{i}") for i in range(2)]
            self.s_w = [Sem(ctx, nc, f"s_w{i}") for i in range(2)]
            self.s_st = [Sem(ctx, nc, f"s_st{i}") for i in range(2)]
            self.s_msk = [Sem(ctx, nc, f"s_msk{i}") for i in range(2)]
            self.s_ctx = [Sem(ctx, nc, f"s_ctx{i}") for i in range(2)]
            self.s_wd = [Sem(ctx, nc, f"s_wd{i}") for i in range(4)]
            self.all_dma_sems = ([self.s_sy, self.s_gp, self.s_cst, self.s_tab] + self.s_ld +
                                 self.s_w + self.s_st + self.s_msk + self.s_ctx + self.s_wd)
            self.block = ctx.enter_context(nc.Block())
            block = self.block

            @block.sync
            def _(sy):
                self.s_cst.inc(sy.dma_start(out=self.idf[:], in_=self.id_f32), 16)
                self.s_cst.inc(sy.dma_start(out=self.qkn_sb[:], in_=self.qkn), 16)
            d_const = self.s_cst.n

            @block.vector
            def _(v):
                v.memset(self.ones_col[:], 1.0)
                v.memset(self.ones_row[:], 1.0)
                v.memset(self.consts[:, 0:1], EPS_LN)
                v.memset(self.consts[:, 1:2], EPS_RMS)
                v.memset(self.consts[:, 2:3], -0.5)
                v.wait_ge(self.s_cst.h, d_const)
                v.tensor_copy(self.idb[:], self.idf[:])
                self.s_dve.inc(v.engine_nop())

            with ExitStack() as l1:
                sb1 = lambda name, shape, dt: l1.enter_context(nc.sbuf_tensor(name, list(shape), dt))
                self.PSF = l1.enter_context(nc.psum_tensor("psf", [128, 3584], F32))
                self.PSB = l1.enter_context(nc.psum_tensor("psb", [128, 1024], BF16))
                self.KT = sb1("KT", [128, 2 * L], BF16)
                self.V_sb = sb1("V_sb", [128, NKT * VP], BF16)
                if self.phases >= 1:
                    self.phase_kv(nc, d_const)
                with ExitStack() as lqa:
                    self.qT_sb = lqa.enter_context(
                        nc.sbuf_tensor("qT_sb", [128, 2 * H * TOK], BF16))
                    if self.phases >= 2:
                        self.phase_q(nc)
                    if self.phases >= 3:
                        self.phase_attn(nc)
            if self.phases >= 4:
                with ExitStack() as l2o:
                    self.PS8 = l2o.enter_context(nc.psum_tensor("ps8", [128, 4096], F32))
                    with ExitStack() as l2:
                        sb2 = lambda name, shape, dt: l2.enter_context(nc.sbuf_tensor(name, list(shape), dt))
                        self.ln2T = sb2("ln2T", [128, NDT * TOK], BF16)
                        self.phase_o(nc, sb2)
                        if self.phases >= 5:
                            self.phase_ffn_a(nc)
                    if self.phases >= 6:
                        self.phase_down(nc)
                    if self.phases >= 7:
                        self.phase_epilogue(nc)

            @block.sync
            def _(sy):
                for s in self.all_dma_sems + [self.s_pe, self.s_dve, self.s_act]:
                    s.wait(sy)
        return nc


    def phase_kv(self, nc, d_const):
        block = self.block
        s_sy, s_gp, s_pe, s_dve, s_act = self.s_sy, self.s_gp, self.s_pe, self.s_dve, self.s_act
        bank = lambda i, off=0, n=512: self.PSF[:, 512 * i + off: 512 * i + off + n]
        rows = self.rows
        with ExitStack() as p1:
            sb1 = lambda name, shape, dt: p1.enter_context(nc.sbuf_tensor(name, list(shape), dt))
            cosT_sb = sb1("cosT_sb", [128, L], F32)
            sinT_sb = sb1("sinT_sb", [128, L], F32)
            x_sl = [sb1(f"x_sl{i}", [128, 4096], F32) for i in range(2)]
            xbf = [sb1(f"xbf{i}", [128, 4096], BF16) for i in range(2)]
            xsq = [sb1(f"xsq{i}", [128, 4096], BF16) for i in range(2)]
            ln1_sl = [sb1(f"ln1_sl{i}", [128, 4096], BF16) for i in range(2)]
            k_sb = sb1("k_sb", [128, 512], F32)
            ksq = sb1("ksq", [128, 512], BF16)
            kn_sb = sb1("kn_sb", [128, 512], F32)
            kt0 = sb1("kt0", [128, 256], F32)
            kt1 = sb1("kt1", [128, 256], F32)
            wk_sb = sb1("wk_sb", [128, NDT * 256], BF16)
            wv_sb = sb1("wv_sb", [128, NDT * 256], BF16)
            KT, V_sb = self.KT, self.V_sb
            mk = {}

            @block.gpsimd
            def _(g):
                s_gp.inc(g.dma_start(
                    out=wk_sb[:].rearrange("p (n c) -> p n c", n=NDT),
                    in_=self.wkT.rearrange("(n p) c -> p n c", p=128)), 16)
                s_gp.inc(g.dma_start(
                    out=wv_sb[:].rearrange("p (n c) -> p n c", n=NDT),
                    in_=self.wvT.rearrange("(n p) c -> p n c", p=128)), 16)
            d_wkv = s_gp.n

            @block.sync
            def _(sy):
                self.s_tab.inc(sy.dma_start(out=cosT_sb[:], in_=self.cosT), 16)
                self.s_tab.inc(sy.dma_start(out=sinT_sb[:], in_=self.sinT), 16)
            d_tab = self.s_tab.n

            @block.vector
            def _(v):
                v.memset(V_sb[:], 0.0)
                v.memset(V_sb[:, 256:NKT * VP:VP], 1.0)
                s_dve.inc(v.engine_nop())

            for sl in range(16):
                p = sl % 2
                tok0 = sl * 256

                @block.sync
                def _(sy, sl=sl, p=p, tok0=tok0):
                    if sl >= 2:
                        s_dve.wait(sy, mk[("apply", sl - 2)])
                    self.s_ld[p].inc(sy.dma_start(
                        out=x_sl[p][:].rearrange("p (n c) -> p n c", n=NDT),
                        in_=self.xT_full.rearrange("(n p) l -> p n l", p=128)
                            [:, :, tok0:tok0 + 256]), 16)
                d_x = self.s_ld[p].n

                @block.vector
                def _(v, d_x=d_x, sl=sl, p=p):
                    v.wait_ge(self.s_ld[p].h, d_x)
                    if sl >= 2:
                        s_pe.wait(v, mk[("stat", sl - 2)])
                    v.tensor_copy(xbf[p][:], x_sl[p][:])
                    s_dve.inc(v.engine_nop())
                d_xbf = s_dve.n

                @block.scalar
                def _(s, d_x=d_x, sl=sl, p=p):
                    s.wait_ge(self.s_ld[p].h, d_x)
                    if sl >= 2:
                        s_pe.wait(s, mk[("stat", sl - 2)])
                    s_act.inc(s.activation(xsq[p][:], x_sl[p][:], AF.Square))
                d_xsq = s_act.n

                @block.tensor
                def _(t, d1=d_xbf, d2=d_xsq, sl=sl, p=p):
                    t = _PE(t)
                    t.wait_ge(s_dve.h, d1)
                    t.wait_ge(s_act.h, d2)
                    if sl >= 1:
                        s_dve.wait(t, mk[("rows", sl - 1)])
                    for dt in range(NDT):
                        t.matmul(bank(3, 0, 256)[0:1, :], self.ones_col[:],
                                 xbf[p][:, dt * 256:(dt + 1) * 256],
                                 start=(dt == 0), stop=(dt == NDT - 1))
                    for dt in range(NDT):
                        t.matmul(bank(3, 256, 256)[0:1, :], self.ones_col[:],
                                 xsq[p][:, dt * 256:(dt + 1) * 256],
                                 start=(dt == 0), stop=(dt == NDT - 1))
                    s_pe.inc(t.last)
                mk[("stat", sl)] = d_stat = s_pe.n

                @block.vector
                def _(v, d=d_stat):
                    v.wait_ge(s_pe.h, d)
                    v.tensor_scalar_mul(rows[0:1, 0:256], bank(3, 0, 256)[0:1, :], 1.0 / D)
                    v.tensor_scalar_mul(rows[0:1, 512:768], bank(3, 256, 256)[0:1, :], 1.0 / D)
                    v.tensor_mul(rows[0:1, 1024:1280], rows[0:1, 0:256], rows[0:1, 0:256])
                    v.tensor_sub(rows[0:1, 1024:1280], rows[0:1, 512:768], rows[0:1, 1024:1280])
                    s_dve.inc(v.engine_nop())
                d_var = s_dve.n

                @block.scalar
                def _(s, d=d_var):
                    s.wait_ge(s_dve.h, d)
                    s.activation(rows[0:1, 1280:1536], rows[0:1, 1024:1280], AF.Ln, bias=EPS_LN)
                    s_act.inc(s.activation(rows[0:1, 1536:1792], rows[0:1, 1280:1536],
                                           AF.Exp, scale=-0.5))
                d_rstd = s_act.n

                @block.vector
                def _(v, d=d_rstd):
                    v.wait_ge(s_act.h, d)
                    v.scalar_tensor_tensor(rows[0:1, 2048:2304], rows[0:1, 0:256], -1.0,
                                           rows[0:1, 1536:1792], op0=ALU.mult, op1=ALU.mult)
                    s_dve.inc(v.engine_nop())
                mk[("rows", sl)] = d_rows = s_dve.n

                @block.tensor
                def _(t, d=d_rows, sl=sl):
                    t = _PE(t)
                    t.wait_ge(s_dve.h, d)
                    if sl >= 2:
                        s_dve.wait(t, mk[("apply", sl - 2)])
                    bb = 5 + (sl % 2)
                    t.matmul(bank(bb, 0, 256), self.ones_row[:], rows[0:1, 1536:1792],
                             start=True, stop=True)
                    t.matmul(bank(bb, 256, 256), self.ones_row[:], rows[0:1, 2048:2304],
                             start=True, stop=True)
                    s_pe.inc(t.last)
                d_bc = s_pe.n

                @block.vector
                def _(v, d=d_bc, sl=sl, p=p):
                    v.wait_ge(s_pe.h, d)
                    if sl >= 2:
                        s_pe.wait(v, mk[("vmm", sl - 2, 1)])
                    bb = 5 + (sl % 2)
                    for dt in range(NDT):
                        cs = slice(dt * 256, (dt + 1) * 256)
                        v.tensor_mul(x_sl[p][:, cs], x_sl[p][:, cs], bank(bb, 0, 256))
                        v.tensor_add(ln1_sl[p][:, cs], x_sl[p][:, cs], bank(bb, 256, 256))
                    s_dve.inc(v.engine_nop())
                mk[("apply", sl)] = d_apply = s_dve.n

                @block.tensor
                def _(t, d=d_apply, sl=sl, p=p):
                    t = _PE(t)
                    t.wait_ge(s_dve.h, d)
                    if sl == 0:
                        t.wait_ge(s_gp.h, d_wkv)
                    if sl >= 1:
                        s_dve.wait(t, mk[("kevac", sl - 1)])
                    for half in range(2):
                        for dt in range(NDT):
                            t.matmul(bank(0, half * 256, 256),
                                     wk_sb[:, dt * 256 + half * 128: dt * 256 + half * 128 + 128],
                                     ln1_sl[p][:, dt * 256:(dt + 1) * 256],
                                     start=(dt == 0), stop=(dt == NDT - 1))
                    s_pe.inc(t.last)
                d_kmm = s_pe.n

                @block.tensor
                def _(t, sl=sl, p=p):
                    t = _PE(t)
                    for tt in range(2):
                        if sl >= 1:
                            s_dve.wait(t, mk[("vevac", sl - 1, tt)])
                        for dt in range(NDT):
                            t.matmul(bank(1 + tt, 0, 256),
                                     ln1_sl[p][:, dt * 256 + tt * 128: dt * 256 + tt * 128 + 128],
                                     wv_sb[:, dt * 256:(dt + 1) * 256],
                                     start=(dt == 0), stop=(dt == NDT - 1))
                        s_pe.inc(t.last)
                        mk[("vmm", sl, tt)] = s_pe.n

                for tt in range(2):
                    @block.vector
                    def _(v, sl=sl, tt=tt):
                        s_pe.wait(v, mk[("vmm", sl, tt)])
                        tile = sl * 2 + tt
                        v.tensor_copy(V_sb[:, tile * VP: tile * VP + 256], bank(1 + tt, 0, 256))
                        s_dve.inc(v.engine_nop())
                        mk[("vevac", sl, tt)] = s_dve.n

                @block.vector
                def _(v, d=d_kmm, sl=sl):
                    v.wait_ge(s_pe.h, d)
                    if sl >= 1:
                        s_act.wait(v, mk[("ksq", sl - 1)])
                    v.tensor_copy(k_sb[:], bank(0, 0, 512))
                    s_dve.inc(v.engine_nop())
                mk[("kevac", sl)] = d_kevac = s_dve.n

                @block.scalar
                def _(s, d=d_kevac):
                    s.wait_ge(s_dve.h, d)
                    s_act.inc(s.activation(ksq[:], k_sb[:], AF.Square))
                mk[("ksq", sl)] = d_ksq = s_act.n

                @block.tensor
                def _(t, d=d_ksq, sl=sl):
                    t = _PE(t)
                    t.wait_ge(s_act.h, d)
                    if sl >= 1:
                        s_dve.wait(t, mk[("krows", sl - 1)])
                    for half in range(2):
                        t.matmul(bank(4, 0, 256)[0:1, :], self.ones_col[:],
                                 ksq[:, half * 256:(half + 1) * 256],
                                 start=(half == 0), stop=(half == 1))
                    s_pe.inc(t.last)
                d_kms = s_pe.n

                @block.vector
                def _(v, d=d_kms):
                    v.wait_ge(s_pe.h, d)
                    v.tensor_scalar_mul(rows[0:1, 2560:2816], bank(4, 0, 256)[0:1, :], 1.0 / HD)
                    s_dve.inc(v.engine_nop())
                mk[("krows", sl)] = d_kvar = s_dve.n

                @block.scalar
                def _(s, d=d_kvar):
                    s.wait_ge(s_dve.h, d)
                    s.activation(rows[0:1, 2816:3072], rows[0:1, 2560:2816], AF.Ln, bias=EPS_RMS)
                    s_act.inc(s.activation(rows[0:1, 3072:3328], rows[0:1, 2816:3072],
                                           AF.Exp, scale=-0.5))
                d_krstd = s_act.n

                @block.tensor
                def _(t, d=d_krstd, sl=sl):
                    t = _PE(t)
                    t.wait_ge(s_act.h, d)
                    if sl >= 1:
                        s_dve.wait(t, mk[("kdone", sl - 1)])
                    t.matmul(bank(4, 256, 256), self.ones_row[:], rows[0:1, 3072:3328],
                             start=True, stop=True)
                    s_pe.inc(t.last)
                d_kbc = s_pe.n

                @block.vector
                def _(v, d=d_kbc, tok0=tok0, sl=sl):
                    v.wait_ge(s_pe.h, d)
                    if sl == 0:
                        v.wait_ge(self.s_tab.h, d_tab)
                    for half in range(2):
                        cs = slice(half * 256, (half + 1) * 256)
                        v.tensor_mul(kn_sb[:, cs], k_sb[:, cs], bank(4, 256, 256))
                        v.tensor_scalar_mul(kn_sb[:, cs], kn_sb[:, cs],
                                            self.qkn_sb[:, 2 + half:3 + half])
                    c = cosT_sb[:, tok0:tok0 + 256]
                    s_ = sinT_sb[:, tok0:tok0 + 256]
                    v.tensor_mul(kt0[:], kn_sb[:, 0:256], c)
                    v.tensor_mul(kt1[:], kn_sb[:, 256:512], s_)
                    v.tensor_sub(KT[:, tok0:tok0 + 256], kt0[:], kt1[:])
                    v.tensor_mul(kt0[:], kn_sb[:, 0:256], s_)
                    v.tensor_mul(kt1[:], kn_sb[:, 256:512], c)
                    v.tensor_add(KT[:, L + tok0: L + tok0 + 256], kt0[:], kt1[:])
                    s_dve.inc(v.engine_nop())
                mk[("kdone", sl)] = s_dve.n

            t_KT = self.tap("KT", [128, 2 * L], BF16)
            t_V = self.tap("V", [128, NKT * VP], BF16)
            if t_KT is not None or t_V is not None:
                @block.sync
                def _(sy):
                    s_dve.wait(sy)
                    if t_KT is not None:
                        s_sy.inc(sy.dma_start(out=t_KT, in_=KT[:]), 16)
                    if t_V is not None:
                        s_sy.inc(sy.dma_start(out=t_V, in_=V_sb[:]), 16)
                    sy.wait_ge(s_sy.h, s_sy.n)

    def phase_q(self, nc):
        qcut = 0
        block = self.block
        s_sy, s_gp, s_pe, s_dve, s_act = self.s_sy, self.s_gp, self.s_pe, self.s_dve, self.s_act
        bank = lambda i, off=0, n=512: self.PSF[:, 512 * i + off: 512 * i + off + n]
        rows = self.rows
        self.join()
        if qcut == 1:
            return
        with ExitStack() as p2:
            sb2 = lambda name, shape, dt: p2.enter_context(nc.sbuf_tensor(name, list(shape), dt))
            co_sb = sb2("co_sb", [128, TOK], F32)
            so_sb = sb2("so_sb", [128, TOK], F32)
            ln1o = sb2("ln1o", [128, NDT * 1024], BF16)
            x_sl = sb2("x_sl2", [128, 2048], F32)
            xbf = sb2("xbf2", [128, 2048], BF16)
            xsq = sb2("xsq2", [128, 2048], BF16)
            wq_sb = [sb2(f"wq_sb{i}", [128, NDT * 128], BF16) for i in range(2)]
            qf = [sb2(f"qf{i}", [128, 1024], F32) for i in range(2)]
            rbc = sb2("rbc", [128, 1024], F32)
            mk = {}

            @block.sync
            def _(sy):
                self.s_tab.inc(sy.dma_start(out=co_sb[:], in_=self.cos_own), 16)
                self.s_tab.inc(sy.dma_start(out=so_sb[:], in_=self.sin_own), 16)
            d_tab = self.s_tab.n

            for th in range(2):
                hb = th * 1024
                for sl in range(8):
                    tok0 = th * 1024 + sl * 128

                    @block.sync
                    def _(sy, sl=sl, th=th, tok0=tok0):
                        if (th, sl) != (0, 0):
                            s_dve.wait(sy, max(mk[("apply",)], mk.get(("qrope", 7), 0)))
                        self.s_ld[0].inc(sy.dma_start(
                            out=x_sl[:].rearrange("p (n c) -> p n c", n=NDT),
                            in_=self.xT_own.rearrange("(n p) l -> p n l", p=128)
                                [:, :, tok0:tok0 + 128]), 16)
                    d_x = self.s_ld[0].n

                    @block.vector
                    def _(v, d=d_x, sl=sl, th=th):
                        v.wait_ge(self.s_ld[0].h, d)
                        if (th, sl) != (0, 0):
                            s_pe.wait(v, max(mk[("stat",)], mk.get(("qms",), 0)))
                        v.tensor_copy(xbf[:], x_sl[:])
                        s_dve.inc(v.engine_nop())
                    d_xbf = s_dve.n

                    @block.scalar
                    def _(s, d=d_x, sl=sl, th=th):
                        s.wait_ge(self.s_ld[0].h, d)
                        if (th, sl) != (0, 0):
                            s_pe.wait(s, mk[("stat",)])
                        s_act.inc(s.activation(xsq[:], x_sl[:], AF.Square))
                    d_xsq = s_act.n

                    @block.tensor
                    def _(t, d1=d_xbf, d2=d_xsq, sl=sl, th=th):
                        t = _PE(t)
                        t.wait_ge(s_dve.h, d1)
                        t.wait_ge(s_act.h, d2)
                        if (th, sl) != (0, 0):
                            s_dve.wait(t, mk[("rows",)])
                        for dt in range(NDT):
                            t.matmul(bank(3, 0, 128)[0:1, :], self.ones_col[:],
                                     xbf[:, dt * 128:(dt + 1) * 128],
                                     start=(dt == 0), stop=(dt == NDT - 1))
                        for dt in range(NDT):
                            t.matmul(bank(3, 128, 128)[0:1, :], self.ones_col[:],
                                     xsq[:, dt * 128:(dt + 1) * 128],
                                     start=(dt == 0), stop=(dt == NDT - 1))
                        s_pe.inc(t.last)
                    mk[("stat",)] = d_stat = s_pe.n

                    @block.vector
                    def _(v, d=d_stat):
                        v.wait_ge(s_pe.h, d)
                        v.tensor_scalar_mul(rows[0:1, 0:128], bank(3, 0, 128)[0:1, :], 1.0 / D)
                        v.tensor_scalar_mul(rows[0:1, 512:640], bank(3, 128, 128)[0:1, :], 1.0 / D)
                        v.tensor_mul(rows[0:1, 1024:1152], rows[0:1, 0:128], rows[0:1, 0:128])
                        v.tensor_sub(rows[0:1, 1024:1152], rows[0:1, 512:640], rows[0:1, 1024:1152])
                        s_dve.inc(v.engine_nop())
                    d_var = s_dve.n

                    @block.scalar
                    def _(s, d=d_var):
                        s.wait_ge(s_dve.h, d)
                        s.activation(rows[0:1, 1280:1408], rows[0:1, 1024:1152], AF.Ln, bias=EPS_LN)
                        s_act.inc(s.activation(rows[0:1, 1536:1664], rows[0:1, 1280:1408],
                                               AF.Exp, scale=-0.5))
                    d_rstd = s_act.n

                    @block.vector
                    def _(v, d=d_rstd):
                        v.wait_ge(s_act.h, d)
                        v.scalar_tensor_tensor(rows[0:1, 2048:2176], rows[0:1, 0:128], -1.0,
                                               rows[0:1, 1536:1664], op0=ALU.mult, op1=ALU.mult)
                        s_dve.inc(v.engine_nop())
                    mk[("rows",)] = d_rows = s_dve.n

                    @block.tensor
                    def _(t, d=d_rows, sl=sl, th=th):
                        t = _PE(t)
                        t.wait_ge(s_dve.h, d)
                        s_dve.wait(t, mk.get(("applyprev", sl % 2), 0))
                        bb = 5 + (sl % 2)
                        t.matmul(bank(bb, 0, 128), self.ones_row[:], rows[0:1, 1536:1664],
                                 start=True, stop=True)
                        t.matmul(bank(bb, 128, 128), self.ones_row[:], rows[0:1, 2048:2176],
                                 start=True, stop=True)
                        s_pe.inc(t.last)
                    d_bc = s_pe.n

                    @block.vector
                    def _(v, d=d_bc, sl=sl, th=th):
                        v.wait_ge(s_pe.h, d)
                        if th == 1:
                            s_pe.wait(v, mk.get(("qmmw", 1), 0))
                        bb = 5 + (sl % 2)
                        for dt in range(NDT):
                            cs = slice(dt * 128, (dt + 1) * 128)
                            ds = slice(dt * 1024 + sl * 128, dt * 1024 + sl * 128 + 128)
                            v.tensor_mul(x_sl[:, cs], x_sl[:, cs], bank(bb, 0, 128))
                            v.tensor_add(ln1o[:, ds], x_sl[:, cs], bank(bb, 128, 128))
                        s_dve.inc(v.engine_nop())
                    mk[("apply",)] = mk[("applyprev", sl % 2)] = s_dve.n
                d_lnown = s_dve.n

                if qcut == 2:
                    continue
                for h in range(H):
                    for half in range(2):
                        ft = 2 * h + half
                        wp = ft % 2

                        @block.gpsimd
                        def _(g, ft=ft, wp=wp, th=th):
                            s_pe.wait(g, mk.get(("qmmw", wp), 0))
                            self.s_w[wp].inc(g.dma_start(
                                out=wq_sb[wp][:].rearrange("p (n c) -> p n c", n=NDT),
                                in_=self.wqT.rearrange("(n p) c -> p n c", p=128)
                                    [:, :, ft * 128:(ft + 1) * 128]), 16)
                        d_w = self.s_w[wp].n

                        @block.tensor
                        def _(t, d=d_w, half=half, wp=wp, th=th, ft=ft):
                            t = _PE(t)
                            t.wait_ge(self.s_w[wp].h, d)
                            if ft == 0:
                                t.wait_ge(s_dve.h, d_lnown)
                            s_dve.wait(t, mk.get(("qevac", half), 0))
                            for dt in range(NDT):
                                for nn in range(2):
                                    t.matmul(bank(half * 2 + nn, 0, 512),
                                             wq_sb[wp][:, dt * 128:(dt + 1) * 128],
                                             ln1o[:, dt * 1024 + nn * 512: dt * 1024 + (nn + 1) * 512],
                                             start=(dt == 0), stop=(dt == NDT - 1))
                            s_pe.inc(t.last)
                        mk[("qmmw", wp)] = d_qmm = s_pe.n

                        @block.vector
                        def _(v, d=d_qmm, half=half):
                            v.wait_ge(s_pe.h, d)
                            v.tensor_copy(qf[half][:, 0:512], bank(half * 2, 0, 512))
                            v.tensor_copy(qf[half][:, 512:1024], bank(half * 2 + 1, 0, 512))
                            s_dve.inc(v.engine_nop())
                        mk[("qevac", half)] = s_dve.n
                    d_qevac = s_dve.n
                    if qcut == 3:
                        continue

                    @block.scalar
                    def _(s, d=d_qevac, h=h):
                        s.wait_ge(s_dve.h, d)
                        s_pe.wait(s, mk.get(("qms",), 0))
                        s.activation(xbf[:, 0:1024], qf[0][:], AF.Square)
                        s_act.inc(s.activation(xbf[:, 1024:2048], qf[1][:], AF.Square))
                    mk[("qsqm",)] = d_qsq = s_act.n

                    @block.tensor
                    def _(t, d=d_qsq, h=h):
                        t = _PE(t)
                        t.wait_ge(s_act.h, d)
                        s_dve.wait(t, mk.get(("qvar",), 0))
                        for cc in range(2):
                            for half in range(2):
                                t.matmul(bank(4 + cc, 0, 512)[0:1, :], self.ones_col[:],
                                         xbf[:, half * 1024 + cc * 512:
                                             half * 1024 + (cc + 1) * 512],
                                         start=(half == 0), stop=(half == 1))
                        s_pe.inc(t.last)
                    mk[("qms",)] = d_qms = s_pe.n

                    @block.vector
                    def _(v, d=d_qms):
                        v.wait_ge(s_pe.h, d)
                        v.tensor_scalar_mul(rows[0:1, 0:512], bank(4, 0, 512)[0:1, :], 1.0 / HD)
                        v.tensor_scalar_mul(rows[0:1, 512:1024], bank(5, 0, 512)[0:1, :], 1.0 / HD)
                        s_dve.inc(v.engine_nop())
                    mk[("qvar",)] = d_qvar = s_dve.n

                    @block.scalar
                    def _(s, d=d_qvar):
                        s.wait_ge(s_dve.h, d)
                        s.activation(rows[0:1, 1024:2048], rows[0:1, 0:1024], AF.Ln, bias=EPS_RMS)
                        s_act.inc(s.activation(rows[0:1, 2048:3072], rows[0:1, 1024:2048],
                                               AF.Exp, scale=-0.5))
                    d_qrstd = s_act.n

                    for cc in range(2):
                        @block.tensor
                        def _(t, cc=cc, d=d_qrstd):
                            t = _PE(t)
                            t.wait_ge(s_act.h, d)
                            s_dve.wait(t, mk.get(("rbcc", 0), 0))
                            s_dve.wait(t, mk.get(("rbcc", 1), 0))
                            t.matmul(bank(6, 0, 512), self.ones_row[:],
                                     rows[0:1, 2048 + cc * 512: 2048 + (cc + 1) * 512],
                                     start=True, stop=True)
                            s_pe.inc(t.last)
                        d_bcmm = s_pe.n

                        @block.vector
                        def _(v, cc=cc, d=d_bcmm):
                            v.wait_ge(s_pe.h, d)
                            v.tensor_copy(rbc[:, cc * 512:(cc + 1) * 512], bank(6, 0, 512))
                            s_dve.inc(v.engine_nop())
                            mk[("rbcc", cc)] = s_dve.n

                    if qcut == 4:
                        continue

                    @block.vector
                    def _(v, h=h, th=th, hb=hb, dq=d_qsq):
                        v.wait_ge(s_act.h, dq)
                        for half in range(2):
                            v.tensor_mul(qf[half][:], qf[half][:], rbc[:])
                            v.tensor_scalar(qf[half][:], qf[half][:],
                                            self.qkn_sb[:, half:half + 1], 1.0 / 16.0,
                                            op0=ALU.mult, op1=ALU.mult)
                        ta = x_sl[:, 0:1024]
                        tb = x_sl[:, 1024:2048]
                        co = co_sb[:, hb:hb + 1024]
                        so = so_sb[:, hb:hb + 1024]
                        o1 = self.qT_sb[:, (2 * h) * TOK + hb: (2 * h) * TOK + hb + 1024]
                        o2 = self.qT_sb[:, (2 * h + 1) * TOK + hb: (2 * h + 1) * TOK + hb + 1024]
                        v.tensor_mul(ta, qf[0][:], co)
                        v.tensor_mul(tb, qf[1][:], so)
                        v.tensor_sub(o1, ta, tb)
                        v.tensor_mul(ta, qf[0][:], so)
                        v.tensor_mul(tb, qf[1][:], co)
                        v.tensor_add(o2, ta, tb)
                        s_dve.inc(v.engine_nop())
                    mk[("qrope", h)] = s_dve.n
                if th == 0:
                    @block.vector
                    def _(v):
                        v.wait_ge(self.s_tab.h, d_tab)
                        s_dve.inc(v.engine_nop())

            t_qT = self.tap("qT", [128, 2 * H * TOK], BF16)
            if t_qT is not None:
                @block.sync
                def _(sy):
                    s_dve.wait(sy)
                    s_sy.inc(sy.dma_start(out=t_qT, in_=self.qT_sb[:]), 16)
                    sy.wait_ge(s_sy.h, s_sy.n)

    def join(self):
        block = self.block
        s_sy, s_gp, s_pe, s_dve, s_act = self.s_sy, self.s_gp, self.s_pe, self.s_dve, self.s_act

        dmas = self.all_dma_sems

        @block.tensor
        def _(t):
            for s in dmas + [s_dve, s_act]:
                s.wait(t)
            s_pe.inc(t.nop())

        @block.vector
        def _(v):
            for s in dmas + [s_pe, s_act]:
                s.wait(v)
            s_dve.inc(v.engine_nop())

        @block.scalar
        def _(s_):
            for s in dmas + [s_pe, s_dve]:
                s.wait(s_)
            s_act.inc(s_.copy(self.ones_row[0:1, 0:1], self.ones_row[0:1, 0:1]))

        @block.sync
        def _(sy):
            for s in dmas + [s_pe, s_dve, s_act]:
                s.wait(sy)

        @block.gpsimd
        def _(g):
            for s in dmas + [s_pe, s_dve, s_act]:
                s.wait(g)

    def phase_attn(self, nc):
        acut = 0
        block = self.block
        s_sy, s_gp, s_pe, s_dve, s_act = self.s_sy, self.s_gp, self.s_pe, self.s_dve, self.s_act
        bank = lambda i, off=0, n=512: self.PSF[:, 512 * i + off: 512 * i + off + n]
        self.join()
        with ExitStack() as p3:
            sb3 = lambda name, shape, dt: p3.enter_context(nc.sbuf_tensor(name, list(shape), dt))
            mask_sb = [sb3(f"mask_sb{i}", [128, 2048], F32) for i in range(2)]
            wT_sb = sb3("wT_sb", [128, 8 * 1024], BF16)
            acc_sb = sb3("acc_sb", [128, 8 * 264], F32)
            rc_sb = sb3("rc_sb", [128, 8], F32)
            ctx_j = sb3("ctx_j", [128, 2048], BF16)
            ctxT_j = [sb3(f"ctxT_j{i}", [128, 2048], BF16) for i in range(2)]
            mk = {}
            KT, V_sb, qT_sb = self.KT, self.V_sb, self.qT_sb
            qcol = lambda h, half, j: qT_sb[:, ((2 * h + half) * NT + j) * 128:
                                            ((2 * h + half) * NT + j) * 128 + 128]

            if acut == 1:
                return
            for j in range(NT):
                jp = j % 2
                nkt = 2 * (j + 1)
                nch = (nkt * 128 + 511) // 512

                @block.sync
                def _(sy, j=j, jp=jp):
                    if j >= 2:
                        s_act.wait(sy, mk[("maskfree", j - 2)])
                    self.s_msk[jp].inc(sy.dma_start(
                        out=mask_sb[jp][:].rearrange("p (i c) -> p i c", i=2),
                        in_=self.masks[j].rearrange("i p c -> p i c")), 16)
                mk[("maskdma", j)] = (jp, self.s_msk[jp].n)

                for h in range(H):
                    for sw0 in range(0, nch, 4):
                        swc = list(range(sw0, min(sw0 + 4, nch)))

                        @block.tensor
                        def _(t, j=j, h=h, sw0=sw0, swc=swc, nkt=nkt):
                            t = _PE(t)
                            if (j, h, sw0) != (0, 0, 0):
                                s_dve.wait(t, mk[("sfree",)])
                            if h == 0 and sw0 == 0 and j >= 1:
                                s_act.wait(t, mk.get(("explast", j - 1), 0))
                            for half in range(2):
                                for c in swc:
                                    w = min(512, nkt * 128 - c * 512)
                                    t.matmul(bank(c - sw0, 0, w),
                                             qcol(h, half, j),
                                             KT[:, half * L + c * 512: half * L + c * 512 + w],
                                             start=(half == 0), stop=(half == 1))
                            s_pe.inc(t.last)
                        d_smm = s_pe.n

                        @block.vector
                        def _(v, d=d_smm, sw0=sw0, swc=swc, nkt=nkt):
                            v.wait_ge(s_pe.h, d)
                            for c in swc:
                                w = min(512, nkt * 128 - c * 512)
                                v.reduce_max(self.mx_cols[:, c:c + 1], bank(c - sw0, 0, w),
                                             axis=AXX)
                            s_dve.inc(v.engine_nop())
                        mk[("sfree",)] = s_dve.n

                    @block.vector
                    def _(v, h=h, nch=nch):
                        v.tensor_reduce(self.mj_cols[:, h:h + 1], self.mx_cols[:, 0:nch],
                                        axis=AXX, op=ALU.max, negate=True)
                        s_dve.inc(v.engine_nop())
                d_mj = s_dve.n

                if acut == 2:
                    mk[("maskfree", j)] = s_act.n
                    continue

                @block.tensor
                def _(t, d=d_mj, j=j):
                    t = _PE(t)
                    t.wait_ge(s_dve.h, d)
                    if j >= 1:
                        s_dve.wait(t, mk[("mrows", j - 1)])
                    for h in range(H):
                        t.matmul(self.PSF[0:1, h * 128:(h + 1) * 128],
                                 self.mj_cols[:, h:h + 1], self.idf[:],
                                 start=True, stop=True)
                    s_pe.inc(t.last)
                d_mjT = s_pe.n

                @block.vector
                def _(v, d=d_mjT):
                    v.wait_ge(s_pe.h, d)
                    v.tensor_copy(self.mrows_sb[:], self.PSF[0:1, 0:1024])
                    s_dve.inc(v.engine_nop())
                mk[("mrows", j)] = d_mrows = s_dve.n

                if acut == 3:
                    mk[("maskfree", j)] = s_act.n
                    continue
                ngrp = (nkt + 7) // 8
                for g in range(ngrp):
                    t0g, t1g = g * 8, min((g + 1) * 8, nkt)
                    for t_ in range(t0g, t1g):
                        slot = t_ % 2

                        @block.tensor
                        def _(t, j=j, t_=t_, slot=slot, d=d_mrows, g=g):
                            t = _PE(t)
                            t.wait_ge(s_dve.h, d)
                            s_act.wait(t, mk.get(("exp", j, t_ - 2), mk.get(("explast", j - 1), 0)))
                            if t_ == 0:
                                mjp, mval = mk[("maskdma", j)]
                                t.wait_ge(self.s_msk[mjp].h, mval)
                            R = self.PSF[:, slot * 1024: slot * 1024 + 1024]
                            for half in range(2):
                                for h in range(H):
                                    t.matmul(R[:, h * 128:(h + 1) * 128],
                                             KT[:, half * L + t_ * 128: half * L + t_ * 128 + 128],
                                             qcol(h, half, j),
                                             start=(half == 0 and h % 4 == 0), stop=False,
                                             skip_group_check=True)
                            for h in range(H):
                                t.matmul(R[:, h * 128:(h + 1) * 128],
                                         self.ones_row[:],
                                         self.mrows_sb[0:1, h * 128:(h + 1) * 128],
                                         start=False, stop=(h % 4 == 3),
                                         skip_group_check=True)
                            s_pe.inc(t.last)
                        d_stmm = s_pe.n

                        masked = (t_ >= nkt - 2)
                        if masked:
                            @block.vector
                            def _(v, d=d_stmm, j=j, jp=jp, t_=t_, slot=slot, nkt=nkt):
                                v.wait_ge(s_pe.h, d)
                                R = self.PSF[:, slot * 1024: slot * 1024 + 1024]
                                mc = (t_ - (nkt - 2)) * 1024
                                v.tensor_add(R, R, mask_sb[jp][:, mc:mc + 1024])
                                s_dve.inc(v.engine_nop())
                            d_mask = s_dve.n

                        @block.scalar
                        def _(s, d=d_stmm, t_=t_, slot=slot, j=j, g=g, masked=masked):
                            s.wait_ge(s_pe.h, d)
                            if masked:
                                s.wait_ge(s_dve.h, s_dve.n)
                            if t_ % 8 == 0:
                                s_pe.wait(s, mk.get(("ctxglast",), 0))
                            R = self.PSF[:, slot * 1024: slot * 1024 + 1024]
                            s_act.inc(s.activation(wT_sb[:, (t_ % 8) * 1024:(t_ % 8) * 1024 + 1024],
                                                   R, AF.Exp))
                        mk[("exp", j, t_)] = s_act.n

                    d_expg = s_act.n
                    if acut == 4:
                        continue
                    for h in range(H):
                        cslot = h % 2

                        @block.tensor
                        def _(t, d=d_expg, h=h, g=g, j=j, t0g=t0g, t1g=t1g, cslot=cslot):
                            t = _PE(t)
                            t.wait_ge(s_act.h, d)
                            s_dve.wait(t, mk.get(("cslot", cslot), 0))
                            C = self.PSF[:, 2048 + cslot * 512: 2048 + cslot * 512 + 257]
                            for t_ in range(t0g, t1g):
                                t.matmul(C, wT_sb[:, (t_ % 8) * 1024 + h * 128:
                                                  (t_ % 8) * 1024 + h * 128 + 128],
                                         V_sb[:, t_ * VP: t_ * VP + 257],
                                         start=(t_ == t0g), stop=(t_ == t1g - 1))
                            s_pe.inc(t.last)
                        d_cmm = s_pe.n
                        if h == H - 1:
                            mk[("ctxg", j, g)] = mk[("ctxglast",)] = s_pe.n

                        @block.vector
                        def _(v, d=d_cmm, h=h, g=g, j=j, cslot=cslot):
                            v.wait_ge(s_pe.h, d)
                            C = self.PSF[:, 2048 + cslot * 512: 2048 + cslot * 512 + 257]
                            A = acc_sb[:, h * 264: h * 264 + 257]
                            if g == 0:
                                v.tensor_copy(A, C)
                            else:
                                v.tensor_add(A, A, C)
                            s_dve.inc(v.engine_nop())
                        mk[("ctxadd", j, g, h)] = mk[("cslot", cslot)] = s_dve.n
                mk[("explast", j)] = s_act.n

                if j == 0:
                    t_w0 = self.tap("wT0", [128, 2048], BF16)
                    t_m0 = self.tap("mrow0", [1, 1024])
                    if t_w0 is not None:
                        @block.sync
                        def _(sy):
                            s_act.wait(sy)
                            s_dve.wait(sy)
                            s_sy.inc(sy.dma_start(out=t_w0, in_=wT_sb[:, 0:2048]), 16)
                            s_sy.inc(sy.dma_start(out=t_m0, in_=self.mrows_sb[:]), 16)
                            sy.wait_ge(s_sy.h, s_sy.n)

                if acut in (4, 5):
                    mk[("explast", j)] = s_act.n
                    mk[("maskfree", j)] = s_act.n
                    continue
                @block.vector
                def _(v, j=j):
                    for h in range(H):
                        v.reciprocal(rc_sb[:, h:h + 1], acc_sb[:, h * 264 + 256: h * 264 + 257])
                    s_dve.inc(v.engine_nop())
                    v.wait_ge(s_dve.h, s_dve.n)
                    for h in range(H):
                        v.tensor_scalar(ctx_j[:, h * 256:(h + 1) * 256],
                                        acc_sb[:, h * 264: h * 264 + 256],
                                        rc_sb[:, h:h + 1], None, op0=ALU.mult)
                    s_dve.inc(v.engine_nop())
                d_ctxn = s_dve.n

                for fb in range(0, 16, 4):
                    gslot = (fb // 4) % 2
                    TB = (self.PSB[:, 0:512] if gslot == 0
                          else self.PSF[:, 3072:3584].bitcast(BF16))

                    @block.tensor
                    def _(t, d=d_ctxn, fb=fb, j=j, gslot=gslot, TB=TB):
                        t = _PE(t)
                        t.wait_ge(s_dve.h, d)
                        s_dve.wait(t, mk.get(("psbev", gslot), 0))
                        for i in range(4):
                            t.transpose(TB[:, i * 128:(i + 1) * 128],
                                        ctx_j[:, (fb + i) * 128:(fb + i) * 128 + 128],
                                        self.idb[:])
                        s_pe.inc(t.last)
                    d_tp = s_pe.n

                    @block.vector
                    def _(v, d=d_tp, fb=fb, j=j, jp=jp, TB=TB):
                        v.wait_ge(s_pe.h, d)
                        if fb == 0 and j >= 2:
                            cjp, cval = mk[("ctxdma", j - 2)]
                            v.wait_ge(self.s_ctx[cjp].h, cval)
                        for i in range(4):
                            v.tensor_copy(ctxT_j[jp][:, (fb + i) * 128:(fb + i) * 128 + 128],
                                          TB[:, i * 128:(i + 1) * 128])
                        s_dve.inc(v.engine_nop())
                    mk[("psbev", gslot)] = s_dve.n
                d_ctxT = s_dve.n

                @block.sync
                def _(sy, d=d_ctxT, j=j, jp=jp):
                    sy.wait_ge(s_dve.h, d)
                    self.s_ctx[jp].inc(sy.dma_start(
                        out=self.ctxT_sc.rearrange("(ft p) q -> p ft q", p=128)
                            [:, :, j * 128:(j + 1) * 128],
                        in_=ctxT_j[jp][:].rearrange("p (ft q) -> p ft q", ft=16)), 16)
                mk[("ctxdma", j)] = (jp, self.s_ctx[jp].n)
                mk[("maskfree", j)] = s_act.n

            t_ctxT = self.tap("ctxT", [D, TOK], BF16)
            if t_ctxT is not None:
                @block.sync
                def _(sy):
                    for s in self.s_ctx:
                        s.wait(sy)
                    s_sy.inc(sy.dma_start(out=t_ctxT, in_=self.ctxT_sc), 16)
                    sy.wait_ge(s_sy.h, s_sy.n)

    def phase_o(self, nc, sb2):
        block = self.block
        s_sy, s_gp, s_pe, s_dve, s_act = self.s_sy, self.s_gp, self.s_pe, self.s_dve, self.s_act
        PS8 = self.PS8
        rows = self.rows
        self.join()
        with ExitStack() as p4:
            sb4 = lambda name, shape, dt: p4.enter_context(nc.sbuf_tensor(name, list(shape), dt))
            ctxT_sb = sb4("ctxT_sb", [128, NDT * TOK], BF16)
            wo_sb = [sb4(f"wo_sb{i}", [128, 2048], BF16) for i in range(2)]
            rdA = [sb4(f"rdA{i}", [128, 2048], F32) for i in range(2)]
            rdB = [sb4("rdB0", [128, 2048], F32)] * 2
            bfc = sb4("bfc", [128, 2048], BF16)
            sqc = sb4("sqc", [128, 2048], BF16)
            bcA = sb4("bcA", [128, 2048], F32)
            bcB = sb4("bcB", [128, 2048], F32)
            mk = {}

            @block.sync
            def _(sy):
                s_sy.inc(sy.dma_start(
                    out=ctxT_sb[:].rearrange("p (ft q) -> p ft q", ft=NDT),
                    in_=self.ctxT_sc.rearrange("(ft p) q -> p ft q", p=128)), 16)
            d_ctxld = s_sy.n
            mk[("sy_only",)] = True

            for dt in range(NDT):
                pp = dt % 2

                @block.gpsimd
                def _(g, dt=dt, pp=pp):
                    if dt >= 2:
                        s_pe.wait(g, mk[("omm", dt - 2)])
                    self.s_w[pp].inc(g.dma_start(
                        out=wo_sb[pp][:].rearrange("p (n c) -> p n c", n=NDT),
                        in_=self.woT.rearrange("(n p) c -> p n c", p=128)
                            [:, :, dt * 128:(dt + 1) * 128]), 16)
                d_w = self.s_w[pp].n

                @block.tensor
                def _(t, d=d_w, dt=dt, pp=pp):
                    t = _PE(t)
                    t.wait_ge(self.s_w[pp].h, d)
                    if dt == 0:
                        t.wait_ge(s_sy.h, d_ctxld)
                    if dt >= 2:
                        s_dve.wait(t, mk[("aevac", dt - 2)])
                    for ft in range(NDT):
                        for nn in range(4):
                            bn = nn ^ 2
                            t.matmul(PS8[:, pp * 2048 + bn * 512: pp * 2048 + (bn + 1) * 512],
                                     wo_sb[pp][:, ft * 128:(ft + 1) * 128],
                                     ctxT_sb[:, ft * TOK + nn * 512: ft * TOK + (nn + 1) * 512],
                                     start=(ft == 0), stop=(ft == NDT - 1))
                    s_pe.inc(t.last)
                mk[("omm", dt)] = d_omm = s_pe.n

                @block.vector
                def _(v, d=d_omm, dt=dt, pp=pp):
                    v.wait_ge(s_pe.h, d)
                    if dt >= 2:
                        v.wait_ge(self.s_st[pp].h, mk[("adma", dt - 2)])
                    for nn in range(4):
                        bn = nn ^ 2
                        v.tensor_copy(rdA[pp][:, nn * 512:(nn + 1) * 512],
                                      PS8[:, pp * 2048 + bn * 512: pp * 2048 + (bn + 1) * 512])
                    s_dve.inc(v.engine_nop())
                mk[("aevac", dt)] = d_aev = s_dve.n

                @block.sync
                def _(sy, d=d_aev, dt=dt, pp=pp):
                    sy.wait_ge(s_dve.h, d)
                    self.s_st[pp].inc(sy.dma_start(
                        out=self.a_sc.rearrange("(n p) q -> p n q", p=128)[:, dt, :],
                        in_=rdA[pp][:]), 16)
                mk[("adma", dt)] = self.s_st[pp].n

            self._stat_pass(nc, self.a_sc, rdA, bfc, sqc, mk, "a")
            self._rows_and_bcast(nc, bcA, bcB, mk, "a")

            for dt in range(NDT):
                pp = dt % 2

                @block.sync
                def _(sy, dt=dt, pp=pp, g1=self.s_dve.n, g2=self.s_act.n):
                    if dt >= 1:
                        s_dve.wait(sy, mk[("x1done", dt - 1)])
                    if dt < 2:
                        sy.wait_ge(s_dve.h, g1)
                        sy.wait_ge(s_act.h, g2)
                    self.s_ld[pp].inc(sy.dma_start(
                        out=rdA[pp][:], in_=self.a_sc.rearrange("(n p) q -> p n q", p=128)[:, dt, :]), 16)
                    self.s_ld[pp].inc(sy.dma_start(
                        out=rdB[pp][:], in_=self.xT_own.rearrange("(n p) q -> p n q", p=128)[:, dt, :]), 16)
                d_ld = self.s_ld[pp].n

                @block.vector
                def _(v, d=d_ld, dt=dt, pp=pp):
                    v.wait_ge(self.s_ld[pp].h, d)
                    if dt >= 1:
                        s_pe.wait(v, mk[("x1stat", dt - 1)])
                    if dt >= 2:
                        v.wait_ge(self.s_st[pp].h, mk[("x1dma", dt - 2)])
                    for cc in range(4):
                        cs = slice(cc * 512, (cc + 1) * 512)
                        v.tensor_mul(rdA[pp][:, cs], rdA[pp][:, cs], bcA[:, cs])
                        v.tensor_add(rdA[pp][:, cs], rdA[pp][:, cs], bcB[:, cs])
                        v.tensor_add(rdA[pp][:, cs], rdA[pp][:, cs], rdB[pp][:, cs])
                    v.tensor_copy(bfc[:], rdA[pp][:])
                    s_dve.inc(v.engine_nop())
                mk[("x1done", dt)] = d_x1 = s_dve.n

                @block.scalar
                def _(s, d=d_x1, dt=dt, pp=pp):
                    s.wait_ge(s_dve.h, d)
                    s_act.inc(s.activation(sqc[:], rdA[pp][:], AF.Square))
                d_sq = s_act.n

                @block.tensor
                def _(t, d1=d_x1, d2=d_sq, dt=dt):
                    t = _PE(t)
                    t.wait_ge(s_dve.h, d1)
                    t.wait_ge(s_act.h, d2)
                    for cc in range(4):
                        t.matmul(PS8[0:1, cc * 512:(cc + 1) * 512], self.ones_col[:],
                                 bfc[:, cc * 512:(cc + 1) * 512],
                                 start=(dt == 0), stop=(dt == NDT - 1))
                        t.matmul(PS8[0:1, 2048 + cc * 512: 2048 + (cc + 1) * 512], self.ones_col[:],
                                 sqc[:, cc * 512:(cc + 1) * 512],
                                 start=(dt == 0), stop=(dt == NDT - 1))
                    s_pe.inc(t.last)
                mk[("x1stat", dt)] = s_pe.n

                @block.sync
                def _(sy, d=d_x1, dt=dt, pp=pp):
                    sy.wait_ge(s_dve.h, d)
                    self.s_st[pp].inc(sy.dma_start(
                        out=self.x1_sc.rearrange("(n p) q -> p n q", p=128)[:, dt, :],
                        in_=rdA[pp][:]), 16)
                mk[("x1dma", dt)] = self.s_st[pp].n

            self._rows_from_ps8(nc, mk, "x1")
            self._bcast_from_rows(nc, bcA, bcB, mk, "x1")

            for dt in range(NDT):
                pp = dt % 2

                @block.sync
                def _(sy, dt=dt, pp=pp, g1=self.s_dve.n, g2=self.s_act.n):
                    if dt >= 2:
                        s_dve.wait(sy, mk[("ln2ap", dt - 2)])
                    else:
                        sy.wait_ge(s_dve.h, g1)
                        sy.wait_ge(s_act.h, g2)
                    self.s_ld[pp].inc(sy.dma_start(
                        out=rdA[pp][:], in_=self.x1_sc.rearrange("(n p) q -> p n q", p=128)[:, dt, :]), 16)
                d_ld = self.s_ld[pp].n

                @block.vector
                def _(v, d=d_ld, dt=dt, pp=pp):
                    v.wait_ge(self.s_ld[pp].h, d)
                    for cc in range(4):
                        cs = slice(cc * 512, (cc + 1) * 512)
                        v.tensor_mul(rdA[pp][:, cs], rdA[pp][:, cs], bcA[:, cs])
                        v.tensor_add(self.ln2T[:, dt * TOK + cc * 512: dt * TOK + (cc + 1) * 512],
                                     rdA[pp][:, cs], bcB[:, cs])
                    s_dve.inc(v.engine_nop())
                mk[("ln2ap", dt)] = s_dve.n

            t_av = self.tap("av", [D, TOK])
            if t_av is not None:
                for dt in range(NDT):
                    @block.sync
                    def _(sy, dt=dt):
                        s_dve.wait(sy)
                        for s in self.s_st + self.s_ld:
                            s.wait(sy)
                        self.s_ld[0].inc(sy.dma_start(
                            out=rdA[0][:], in_=self.a_sc.rearrange("(n p) q -> p n q", p=128)[:, dt, :]), 16)
                        sy.wait_ge(self.s_ld[0].h, self.s_ld[0].n)
                        self.s_ld[0].inc(sy.dma_start(
                            out=t_av.rearrange("(n p) q -> p n q", p=128)[:, dt, :], in_=rdA[0][:]), 16)
                        sy.wait_ge(self.s_ld[0].h, self.s_ld[0].n)
            t_x1v = self.tap("x1v", [D, TOK])
            if t_x1v is not None:
                for dt in range(NDT):
                    @block.sync
                    def _(sy, dt=dt):
                        s_dve.wait(sy)
                        for s in self.s_st + self.s_ld:
                            s.wait(sy)
                        self.s_ld[0].inc(sy.dma_start(
                            out=rdA[0][:], in_=self.x1_sc.rearrange("(n p) q -> p n q", p=128)[:, dt, :]), 16)
                        sy.wait_ge(self.s_ld[0].h, self.s_ld[0].n)
                        self.s_ld[0].inc(sy.dma_start(
                            out=t_x1v.rearrange("(n p) q -> p n q", p=128)[:, dt, :], in_=rdA[0][:]), 16)
                        sy.wait_ge(self.s_ld[0].h, self.s_ld[0].n)
            t_x1 = self.tap("x1", [D, TOK])
            t_ln2 = self.tap("ln2T", [128, NDT * TOK], BF16)
            if t_x1 is not None or t_ln2 is not None:
                @block.sync
                def _(sy):
                    s_dve.wait(sy)
                    for s in self.s_st:
                        s.wait(sy)
                    if t_x1 is not None:
                        s_sy.inc(sy.dma_start(out=t_x1, in_=self.x1_sc), 16)
                    if t_ln2 is not None:
                        s_sy.inc(sy.dma_start(out=t_ln2, in_=self.ln2T[:]), 16)
                    sy.wait_ge(s_sy.h, s_sy.n)

    def _stat_pass(self, nc, src_sc, rd, bfc, sqc, mk, tag):
        block = self.block
        s_sy, s_pe, s_dve, s_act = self.s_sy, self.s_pe, self.s_dve, self.s_act
        PS8 = self.PS8
        for dt in range(NDT):
            pp = dt % 2

            @block.sync
            def _(sy, dt=dt, pp=pp, g1=self.s_dve.n, g2=self.s_act.n):
                if dt >= 2:
                    s_act.wait(sy, mk[(tag + "sq", dt - 2)])
                    s_dve.wait(sy, mk[(tag + "bf", dt - 2)])
                else:
                    sy.wait_ge(s_dve.h, g1)
                    sy.wait_ge(s_act.h, g2)
                self.s_ld[pp].inc(sy.dma_start(
                    out=rd[pp][:], in_=src_sc.rearrange("(n p) q -> p n q", p=128)[:, dt, :]), 16)
            d_ld = self.s_ld[pp].n

            @block.vector
            def _(v, d=d_ld, dt=dt, pp=pp):
                v.wait_ge(self.s_ld[pp].h, d)
                if dt >= 1:
                    s_pe.wait(v, mk[(tag + "stat", dt - 1)])
                v.tensor_copy(bfc[:], rd[pp][:])
                s_dve.inc(v.engine_nop())
            mk[(tag + "bf", dt)] = d_bf = s_dve.n

            @block.scalar
            def _(s, d=d_ld, dt=dt, pp=pp):
                s.wait_ge(self.s_ld[pp].h, d)
                if dt >= 1:
                    s_pe.wait(s, mk[(tag + "stat", dt - 1)])
                s_act.inc(s.activation(sqc[:], rd[pp][:], AF.Square))
            mk[(tag + "sq", dt)] = d_sq = s_act.n

            @block.tensor
            def _(t, d1=d_bf, d2=d_sq, dt=dt):
                t = _PE(t)
                t.wait_ge(s_dve.h, d1)
                t.wait_ge(s_act.h, d2)
                for cc in range(4):
                    t.matmul(PS8[0:1, cc * 512:(cc + 1) * 512], self.ones_col[:],
                             bfc[:, cc * 512:(cc + 1) * 512],
                             start=(dt == 0), stop=(dt == NDT - 1))
                    t.matmul(PS8[0:1, 2048 + cc * 512: 2048 + (cc + 1) * 512], self.ones_col[:],
                             sqc[:, cc * 512:(cc + 1) * 512],
                             start=(dt == 0), stop=(dt == NDT - 1))
                s_pe.inc(t.last)
            mk[(tag + "stat", dt)] = s_pe.n

        self._rows_from_ps8(nc, mk, tag)

    def _rows_from_ps8(self, nc, mk, tag):
        block = self.block
        s_pe, s_dve, s_act = self.s_pe, self.s_dve, self.s_act
        PS8 = self.PS8
        rows = self.rows
        d_stat = s_pe.n

        @block.vector
        def _(v, d=d_stat):
            v.wait_ge(s_pe.h, d)
            for cc in range(4):
                v.tensor_scalar_mul(rows[0:1, cc * 512:(cc + 1) * 512],
                                    PS8[0:1, cc * 512:(cc + 1) * 512], 1.0 / D)
            for cc in range(4):
                v.tensor_scalar_mul(rows[0:1, 2048 + cc * 512: 2048 + (cc + 1) * 512],
                                    PS8[0:1, 2048 + cc * 512: 2048 + (cc + 1) * 512], 1.0 / D)
            s_dve.inc(v.engine_nop())
        d_r1 = s_dve.n

        @block.vector
        def _(v, d=d_r1):
            v.tensor_mul(PS8[0:1, 0:2048], rows[0:1, 0:2048], rows[0:1, 0:2048])
            v.tensor_sub(rows[0:1, 2048:4096], rows[0:1, 2048:4096], PS8[0:1, 0:2048])
            s_dve.inc(v.engine_nop())
        d_var = s_dve.n

        @block.scalar
        def _(s, d=d_var):
            s.wait_ge(s_dve.h, d)
            s.activation(PS8[0:1, 0:2048], rows[0:1, 2048:4096], AF.Ln, bias=EPS_LN)
            s_act.inc(s.activation(rows[0:1, 2048:4096], PS8[0:1, 0:2048],
                                   AF.Exp, scale=-0.5))
        d_rstd = s_act.n

        @block.vector
        def _(v, d=d_rstd):
            v.wait_ge(s_act.h, d)
            v.scalar_tensor_tensor(rows[0:1, 0:2048], rows[0:1, 0:2048], -1.0,
                                   rows[0:1, 2048:4096], op0=ALU.mult, op1=ALU.mult)
            s_dve.inc(v.engine_nop())
        mk[(tag + "rows",)] = s_dve.n

    def _rows_and_bcast(self, nc, bcA, bcB, mk, tag):
        self._bcast_from_rows(nc, bcA, bcB, mk, tag)

    def _bcast_from_rows(self, nc, bcA, bcB, mk, tag):
        block = self.block
        s_pe, s_dve = self.s_pe, self.s_dve
        PS8 = self.PS8
        rows = self.rows
        d_rows = mk[(tag + "rows",)]
        for cc in range(4):
            @block.tensor
            def _(t, cc=cc, d=d_rows):
                t = _PE(t)
                t.wait_ge(s_dve.h, d)
                if cc >= 1:
                    s_dve.wait(t, mk[(tag + "bcc", cc - 1)])
                t.matmul(PS8[:, 0:512], self.ones_row[:],
                         rows[0:1, 2048 + cc * 512: 2048 + (cc + 1) * 512],
                         start=True, stop=True)
                t.matmul(PS8[:, 512:1024], self.ones_row[:],
                         rows[0:1, cc * 512:(cc + 1) * 512],
                         start=True, stop=True)
                s_pe.inc(t.last)
            d_mm = s_pe.n

            @block.vector
            def _(v, cc=cc, d=d_mm):
                v.wait_ge(s_pe.h, d)
                v.tensor_copy(bcA[:, cc * 512:(cc + 1) * 512], PS8[:, 0:512])
                v.tensor_copy(bcB[:, cc * 512:(cc + 1) * 512], PS8[:, 512:1024])
                s_dve.inc(v.engine_nop())
            mk[(tag + "bcc", cc)] = s_dve.n

    def phase_ffn_a(self, nc):
        block = self.block
        s_sy, s_gp, s_pe, s_dve, s_act = self.s_sy, self.s_gp, self.s_pe, self.s_dve, self.s_act
        PS8 = self.PS8
        self.join()
        with ExitStack() as p5:
            sb5 = lambda name, shape, dt: p5.enter_context(nc.sbuf_tensor(name, list(shape), dt))
            wg_sb = [sb5(f"wg_sb{i}", [128, 2048], BF16) for i in range(2)]
            wu_sb = [sb5(f"wu_sb{i}", [128, 2048], BF16) for i in range(2)]
            silu_sb = [sb5(f"silu_sb{i}", [128, 1024], BF16) for i in range(2)]
            gated_sb = [sb5(f"gated_sb{i}", [128, 2048], BF16) for i in range(2)]
            mk = {}
            for fs in range(64):
                pw = fs % 2

                @block.gpsimd
                def _(g, fs=fs, pw=pw):
                    if fs >= 2:
                        s_pe.wait(g, mk[("gumm", fs - 2, 1)])
                    self.s_w[pw].inc(g.dma_start(
                        out=wg_sb[pw][:].rearrange("p (n c) -> p n c", n=NDT),
                        in_=self.wgT.rearrange("(n p) c -> p n c", p=128)
                            [:, :, fs * 128:(fs + 1) * 128]), 16)
                    self.s_w[pw].inc(g.dma_start(
                        out=wu_sb[pw][:].rearrange("p (n c) -> p n c", n=NDT),
                        in_=self.wuT.rearrange("(n p) c -> p n c", p=128)
                            [:, :, fs * 128:(fs + 1) * 128]), 16)
                d_w = self.s_w[pw].n

                for th in range(2):
                    st = (fs * 2 + th) % 2

                    @block.tensor
                    def _(t, d=d_w, fs=fs, th=th, st=st, pw=pw):
                        t = _PE(t)
                        t.wait_ge(self.s_w[pw].h, d)
                        k = (fs * 2 + th)
                        if k >= 2:
                            s_dve.wait(t, mk[("guev", k - 2)])
                        for dt in range(NDT):
                            for nn in range(2):
                                rhs = self.ln2T[:, dt * TOK + th * 1024 + nn * 512:
                                                dt * TOK + th * 1024 + (nn + 1) * 512]
                                t.matmul(PS8[:, st * 2048 + nn * 512: st * 2048 + (nn + 1) * 512],
                                         wg_sb[pw][:, dt * 128:(dt + 1) * 128], rhs,
                                         start=(dt == 0), stop=(dt == NDT - 1))
                                t.matmul(PS8[:, st * 2048 + 1024 + nn * 512:
                                             st * 2048 + 1024 + (nn + 1) * 512],
                                         wu_sb[pw][:, dt * 128:(dt + 1) * 128], rhs,
                                         start=(dt == 0), stop=(dt == NDT - 1))
                        s_pe.inc(t.last)
                    mk[("gumm", fs, th)] = d_mm = s_pe.n

                    @block.scalar
                    def _(s, d=d_mm, st=st, th=th, fs=fs):
                        s.wait_ge(s_pe.h, d)
                        if fs >= 1:
                            s_dve.wait(s, mk.get(("guev", (fs - 1) * 2 + th), 0))
                        s_act.inc(s.activation(silu_sb[th][:], PS8[:, st * 2048: st * 2048 + 1024],
                                               AF.Silu))
                    d_silu = s_act.n

                    @block.vector
                    def _(v, d=d_silu, fs=fs, th=th, st=st):
                        v.wait_ge(s_act.h, d)
                        if fs >= 2 and th == 0:
                            v.wait_ge(self.s_st[fs % 2].h, mk[("gdma", fs - 2)])
                        v.tensor_mul(gated_sb[fs % 2][:, th * 1024:(th + 1) * 1024],
                                     PS8[:, st * 2048 + 1024: st * 2048 + 2048], silu_sb[th][:])
                        s_dve.inc(v.engine_nop())
                    mk[("guev", fs * 2 + th)] = s_dve.n
                d_gu = s_dve.n

                @block.sync
                def _(sy, d=d_gu, fs=fs):
                    sy.wait_ge(s_dve.h, d)
                    self.s_st[fs % 2].inc(sy.dma_start(
                        out=self.gated_sc.rearrange("(n p) q -> p n q", p=128)[:, fs, :],
                        in_=gated_sb[fs % 2][:]), 16)
                mk[("gdma", fs)] = self.s_st[fs % 2].n

            t_g = self.tap("gated", [FF, TOK], BF16)
            if t_g is not None:
                @block.sync
                def _(sy):
                    for s in self.s_st:
                        s.wait(sy)
                    s_sy.inc(sy.dma_start(out=t_g, in_=self.gated_sc), 16)
                    sy.wait_ge(s_sy.h, s_sy.n)

    def phase_down(self, nc):
        block = self.block
        s_sy, s_gp, s_pe, s_dve, s_act = self.s_sy, self.s_gp, self.s_pe, self.s_dve, self.s_act
        PS8 = self.PS8
        self.join()
        with ExitStack() as p6:
            sb6 = lambda name, shape, dt: p6.enter_context(nc.sbuf_tensor(name, list(shape), dt))
            wd_sb = sb6("wd_sb", [128, 64 * 1024], BF16)
            gch = [sb6(f"gch{i}", [128, 16 * 512], BF16) for i in range(2)]
            dn_ev = [sb6("dn_ev0", [128, 4096], F32)] * 2
            mk = {}
            for dg in range(2):
                for piece in range(4):
                    @block.gpsimd
                    def _(g, dg=dg, piece=piece):
                        if dg == 1:
                            s_pe.wait(g, mk[("dmm", 0, 3, 3)])
                        self.s_wd[piece].inc(g.dma_start(
                            out=wd_sb[:, piece * 16 * 1024:(piece + 1) * 16 * 1024]
                                .rearrange("p (n c) -> p n c", n=16),
                            in_=self.wdT.rearrange("(n p) c -> p n c", p=128)
                                [:, piece * 16:(piece + 1) * 16, dg * 1024:(dg + 1) * 1024]), 16)
                    mk[("wdp", dg, piece)] = self.s_wd[piece].n

                for tq in range(4):
                    for fsg in range(4):
                        pg = (tq * 4 + fsg) % 2

                        @block.sync
                        def _(sy, dg=dg, tq=tq, fsg=fsg, pg=pg):
                            s_pe.wait(sy, mk.get(("gchp", pg), 0))
                            self.s_ld[pg].inc(sy.dma_start(
                                out=gch[pg][:].rearrange("p (n c) -> p n c", n=16),
                                in_=self.gated_sc.rearrange("(n p) q -> p n q", p=128)
                                    [:, fsg * 16:(fsg + 1) * 16, tq * 512:(tq + 1) * 512]), 16)
                        d_g = self.s_ld[pg].n

                        @block.tensor
                        def _(t, d=d_g, dg=dg, tq=tq, fsg=fsg, pg=pg):
                            t = _PE(t)
                            t.wait_ge(self.s_ld[pg].h, d)
                            t.wait_ge(self.s_wd[fsg].h, mk[("wdp", dg, fsg)])
                            if fsg == 0 and (dg, tq) != (0, 0):
                                s_dve.wait(t, mk.get(("dnev", dg, tq - 1), mk.get(("dnev", dg - 1, 3), 0)))
                            for fs in range(16):
                                for ds in range(8):
                                    t.matmul(PS8[:, ds * 512:(ds + 1) * 512],
                                             wd_sb[:, (fsg * 16 + fs) * 1024 + ds * 128:
                                                   (fsg * 16 + fs) * 1024 + ds * 128 + 128],
                                             gch[pg][:, fs * 512:(fs + 1) * 512],
                                             start=(fsg == 0 and fs == 0),
                                             stop=(fsg == 3 and fs == 15))
                            s_pe.inc(t.last)
                        mk[("dmm", dg, tq, fsg)] = mk[("gchp", pg)] = s_pe.n
                    d_dmm = s_pe.n

                    @block.vector
                    def _(v, d=d_dmm, dg=dg, tq=tq):
                        v.wait_ge(s_pe.h, d)
                        v.wait_ge(self.s_st[0].h, mk.get(("dndmap", 0), 0))
                        v.wait_ge(self.s_st[1].h, mk.get(("dndmap", 1), 0))
                        v.tensor_copy(dn_ev[tq % 2][:], PS8[:, 0:4096])
                        s_dve.inc(v.engine_nop())
                    mk[("dnevlast",)] = d_ev = s_dve.n

                    @block.sync
                    def _(sy, d=d_ev, dg=dg, tq=tq):
                        sy.wait_ge(s_dve.h, d)
                        self.s_st[tq % 2].inc(sy.dma_start(
                            out=self.down_sc.rearrange("(a p) t -> p a t", p=128)
                                [:, dg * 8:(dg + 1) * 8, tq * 512:(tq + 1) * 512],
                            in_=dn_ev[tq % 2][:].rearrange("p (a t) -> p a t", a=8)), 16)
                    mk[("dndmap", tq % 2)] = self.s_st[tq % 2].n

    def phase_epilogue(self, nc):
        block = self.block
        s_sy, s_gp, s_pe, s_dve, s_act = self.s_sy, self.s_gp, self.s_pe, self.s_dve, self.s_act
        PS8 = self.PS8
        rows = self.rows
        self.join()
        with ExitStack() as p7:
            sb7 = lambda name, shape, dt: p7.enter_context(nc.sbuf_tensor(name, list(shape), dt))
            dn_rd = [sb7(f"dn_rd{i}", [128, 8192], F32) for i in range(2)]
            x1_rd = [sb7(f"x1_rd{i}", [128, 8192], F32) for i in range(2)]
            bfc = sb7("bfc7", [128, 8192], BF16)
            sqc = sb7("sqc7", [128, 8192], BF16)
            tE = sb7("tE", [128, 512], F32)
            mk = {}
            for tq in range(4):
                pq = tq % 2

                @block.sync
                def _(sy, tq=tq, pq=pq):
                    if tq >= 2:
                        s_dve.wait(sy, mk[("outdone", tq - 2)])
                    self.s_ld[pq].inc(sy.dma_start(
                        out=dn_rd[pq][:].rearrange("p (a t) -> p a t", a=NDT),
                        in_=self.down_sc.rearrange("(a p) t -> p a t", p=128)
                            [:, :, tq * 512:(tq + 1) * 512]), 16)
                    self.s_ld[pq].inc(sy.dma_start(
                        out=x1_rd[pq][:].rearrange("p (a t) -> p a t", a=NDT),
                        in_=self.x1_sc.rearrange("(a p) t -> p a t", p=128)
                            [:, :, tq * 512:(tq + 1) * 512]), 16)
                d_ld = self.s_ld[pq].n

                @block.vector
                def _(v, d=d_ld, tq=tq, pq=pq):
                    v.wait_ge(self.s_ld[pq].h, d)
                    if tq >= 1:
                        s_pe.wait(v, mk[("estat", tq - 1)])
                    v.tensor_copy(bfc[:], dn_rd[pq][:])
                    s_dve.inc(v.engine_nop())
                d_bf = s_dve.n

                @block.scalar
                def _(s, d=d_ld, tq=tq, pq=pq):
                    s.wait_ge(self.s_ld[pq].h, d)
                    if tq >= 1:
                        s_pe.wait(s, mk[("estat", tq - 1)])
                    s_act.inc(s.activation(sqc[:], dn_rd[pq][:], AF.Square))
                d_sq = s_act.n

                @block.tensor
                def _(t, d1=d_bf, d2=d_sq, tq=tq):
                    t = _PE(t)
                    t.wait_ge(s_dve.h, d1)
                    t.wait_ge(s_act.h, d2)
                    if tq >= 1:
                        s_dve.wait(t, mk[("erows", tq - 1)])
                    for dt in range(NDT):
                        t.matmul(PS8[0:1, 0:512], self.ones_col[:],
                                 bfc[:, dt * 512:(dt + 1) * 512],
                                 start=(dt == 0), stop=(dt == NDT - 1))
                        t.matmul(PS8[0:1, 512:1024], self.ones_col[:],
                                 sqc[:, dt * 512:(dt + 1) * 512],
                                 start=(dt == 0), stop=(dt == NDT - 1))
                    s_pe.inc(t.last)
                mk[("estat", tq)] = d_stat = s_pe.n

                @block.vector
                def _(v, d=d_stat):
                    v.wait_ge(s_pe.h, d)
                    v.tensor_scalar_mul(rows[0:1, 0:512], PS8[0:1, 0:512], 1.0 / D)
                    v.tensor_scalar_mul(rows[0:1, 512:1024], PS8[0:1, 512:1024], 1.0 / D)
                    v.tensor_mul(rows[0:1, 1024:1536], rows[0:1, 0:512], rows[0:1, 0:512])
                    v.tensor_sub(rows[0:1, 1024:1536], rows[0:1, 512:1024], rows[0:1, 1024:1536])
                    s_dve.inc(v.engine_nop())
                d_var = s_dve.n

                @block.scalar
                def _(s, d=d_var):
                    s.wait_ge(s_dve.h, d)
                    s.activation(rows[0:1, 1536:2048], rows[0:1, 1024:1536], AF.Ln, bias=EPS_LN)
                    s_act.inc(s.activation(rows[0:1, 2048:2560], rows[0:1, 1536:2048],
                                           AF.Exp, scale=-0.5))
                d_rstd = s_act.n

                @block.vector
                def _(v, d=d_rstd):
                    v.wait_ge(s_act.h, d)
                    v.scalar_tensor_tensor(rows[0:1, 2560:3072], rows[0:1, 0:512], -1.0,
                                           rows[0:1, 2048:2560], op0=ALU.mult, op1=ALU.mult)
                    s_dve.inc(v.engine_nop())
                mk[("erows", tq)] = d_rows = s_dve.n

                @block.tensor
                def _(t, d=d_rows, tq=tq):
                    t = _PE(t)
                    t.wait_ge(s_dve.h, d)
                    if tq >= 1:
                        s_dve.wait(t, mk[("ebc", tq - 1)])
                    t.matmul(PS8[:, 1024:1536], self.ones_row[:], rows[0:1, 2048:2560],
                             start=True, stop=True)
                    t.matmul(PS8[:, 1536:2048], self.ones_row[:], rows[0:1, 2560:3072],
                             start=True, stop=True)
                    s_pe.inc(t.last)
                d_bc = s_pe.n

                @block.vector
                def _(v, d=d_bc, tq=tq, pq=pq):
                    v.wait_ge(s_pe.h, d)
                    for dt in range(NDT):
                        cs = slice(dt * 512, (dt + 1) * 512)
                        v.tensor_mul(tE[:], dn_rd[pq][:, cs], PS8[:, 1024:1536])
                        v.tensor_add(tE[:], tE[:], PS8[:, 1536:2048])
                        v.tensor_add(x1_rd[pq][:, cs], tE[:], x1_rd[pq][:, cs])
                    s_dve.inc(v.engine_nop())
                mk[("ebc", tq)] = mk[("outdone", tq)] = d_out = s_dve.n

                @block.sync
                def _(sy, d=d_out, tq=tq, pq=pq):
                    sy.wait_ge(s_dve.h, d)
                    s_sy.inc(sy.dma_start(
                        out=self.outT.rearrange("(a p) t -> p a t", p=128)
                            [:, :, tq * 512:(tq + 1) * 512],
                        in_=x1_rd[pq][:].rearrange("p (a t) -> p a t", a=NDT)), 16)





def own_indices(side):
    return np.concatenate([np.arange((2 * j + side) * 128, (2 * j + side) * 128 + 128)
                           for j in range(NT)])


def rope_tables():
    i = np.arange(128, dtype=np.float64)
    inv = 1.0 / (1e6 ** (2.0 * i / 256.0))
    pos = np.arange(L, dtype=np.float64)
    fr = inv[:, None] * pos[None, :]
    return np.cos(fr).astype(np.float32), np.sin(fr).astype(np.float32)


def make_masks(side):
    m = np.zeros((NT, 2, 128, 1024), np.float32)
    kp = np.arange(128)
    qi = np.arange(128)
    for j in range(NT):
        qg = (2 * j + side) * 128 + qi
        for i in range(2):
            kg = (2 * j + i) * 128 + kp
            tile = np.where(kg[:, None] <= qg[None, :], 0.0, -1e30).astype(np.float32)
            m[j, i] = np.tile(tile, (1, 8))
    return m


def core_inputs(inp, core):
    b, s = core // 2, core % 2
    x = np.asarray(inp["L0_attn_x"])
    xT = np.ascontiguousarray(x[b].T)
    oi = own_indices(s)
    cosT, sinT = rope_tables()
    T = lambda a: np.ascontiguousarray(np.asarray(a).T)
    qn = np.asarray(inp["L0_attn_q_norm"]); kn = np.asarray(inp["L0_attn_k_norm"])
    d = {
        "xT_full": xT,
        "xT_own": np.ascontiguousarray(xT[:, oi]),
        "wqT": T(inp["L0_attn_q_proj"]),
        "wkT": T(inp["L0_attn_k_proj"]),
        "wvT": T(inp["L0_attn_v_proj"]),
        "woT": T(inp["L0_attn_o_proj"]),
        "wgT": T(inp["L0_ffn_w_gate"]),
        "wuT": T(inp["L0_ffn_w_up"]),
        "wdT": np.ascontiguousarray(np.asarray(inp["L0_ffn_w_down"]).T),
        "qkn": np.ascontiguousarray(np.stack([qn[:128], qn[128:], kn[:128], kn[128:]], axis=1)).astype(np.float32),
        "cosT": cosT, "sinT": sinT,
        "cos_own": np.ascontiguousarray(cosT[:, oi]),
        "sin_own": np.ascontiguousarray(sinT[:, oi]),
        "masks": make_masks(s),
        "id_f32": np.eye(128, dtype=np.float32),
    }
    return {k: (v.astype(np.float32) if v.dtype == np.float64 else v) for k, v in d.items()}


def assemble_out(results):
    out = np.empty((B, L, D), np.float32)
    for core, r in enumerate(results):
        b, s = core // 2, core % 2
        oi = own_indices(s)
        out[b, oi, :] = r["outT"].T
    return out


_CACHE = {}


def kernel(**inputs):
    key = "nc"
    if key not in _CACHE:
        m = MK(phases=9)
        _CACHE[key] = m.build()
    nc = _CACHE[key]
    in_maps = [core_inputs(inputs, c) for c in range(8)]
    res = run_bass_kernel_spmd(nc, in_maps, core_ids=list(range(8)))
    return assemble_out(res.results)


# version 04
# speedup vs baseline: 1.1343x; 1.1343x over previous
import numpy as np
import ml_dtypes
from contextlib import ExitStack
import concourse.bass as bass
import concourse.mybir as mybir
from concourse.bass_utils import run_bass_kernel_spmd

F32 = mybir.dt.float32
BF16 = mybir.dt.bfloat16
AF = mybir.ActivationFunctionType
ALU = mybir.AluOpType
AXX = mybir.AxisListType.X

B, L, D, HD, H, FF = 4, 4096, 2048, 256, 8, 8192
NT = 16
NDT = 16
NKT = 32
TOK = 2048
EPS_LN = 1e-5
EPS_RMS = 1e-6
VP = 272


class Sem:
    def __init__(self, ctx, nc, name):
        self.h = ctx.enter_context(nc.semaphore(name))
        self.n = 0

    def inc(self, instr, k=1):
        instr.then_inc(self.h, k)
        self.n += k
        return instr

    def wait(self, eng, val=None):
        eng.wait_ge(self.h, self.n if val is None else val)


class _PE:

    def __init__(self, t):
        self._t = t
        self.last = None

    def __getattr__(self, name):
        return getattr(self._t, name)

    def matmul(self, *a, **k):
        self.last = self._t.matmul(*a, **k)
        return self.last

    def transpose(self, *a, **k):
        self.last = self._t.transpose(*a, **k)
        return self.last


class MK:
    def __init__(self, phases=9, taps=(), jcap=NT):
        self.phases = phases
        self.taps = taps
        self.jcap = jcap
        self.tap_out = {}
        nc = self.nc = bass.Bass("TRN2", target_bir_lowering=False,
                                 detect_race_conditions=False)
        ap = lambda name, shape, dt=F32: nc.dram_tensor(name, list(shape), dt, kind="ExternalInput").ap()
        self.xT_full = ap("xT_full", [D, L])
        self.xT_own = ap("xT_own", [D, TOK])
        self.wqT = ap("wqT", [D, D])
        self.wkT = ap("wkT", [D, HD])
        self.wvT = ap("wvT", [D, HD])
        self.woT = ap("woT", [D, D])
        self.wgT = ap("wgT", [D, FF])
        self.wuT = ap("wuT", [D, FF])
        self.wdT = ap("wdT", [FF, D])
        self.qkn = ap("qkn", [128, 4])
        self.cosT = ap("cosT", [128, L])
        self.sinT = ap("sinT", [128, L])
        self.cos_own = ap("cos_own", [128, TOK])
        self.sin_own = ap("sin_own", [128, TOK])
        self.masks = ap("masks", [NT, 128, 256], BF16)
        self.id_f32 = ap("id_f32", [128, 128])
        self.outT = nc.dram_tensor("outT", [D, TOK], F32, kind="ExternalOutput").ap()
        self.ctxT_sc = nc.dram_tensor("ctxT_sc", [D, TOK], BF16, kind="Internal").ap()
        self.a_sc = nc.dram_tensor("a_sc", [D, TOK], F32, kind="Internal").ap()
        self.x1_sc = nc.dram_tensor("x1_sc", [D, TOK], F32, kind="Internal").ap()
        self.gated_sc = nc.dram_tensor("gated_sc", [FF, TOK], BF16, kind="Internal").ap()
        self.down_sc = nc.dram_tensor("down_sc", [D, TOK], F32, kind="Internal").ap()

    def tap(self, name, shape, dt=F32):
        if name in self.taps:
            t = self.nc.dram_tensor("tap_" + name, list(shape), dt, kind="ExternalOutput").ap()
            self.tap_out[name] = t
            return t
        return None

    def build(self):
        nc = self.nc
        ctx = ExitStack()
        with ctx:
            sb = lambda name, shape, dt: ctx.enter_context(nc.sbuf_tensor(name, list(shape), dt))
            self.idb = sb("idb", [128, 128], BF16)
            self.idf = sb("idf", [128, 128], F32)
            self.qkn_sb = sb("qkn_sb", [128, 4], F32)
            self.ones_col = sb("ones_col", [128, 1], BF16)
            self.ones_row = sb("ones_row", [1, 128], F32)
            self.rows = sb("rows", [1, 4096], F32)
            self.mx_cols = sb("mx_cols", [128, 8], F32)
            self.mj_cols = sb("mj_cols", [128, 8], F32)
            self.mrows_sb = sb("mrows_sb", [1, 1024], F32)
            self.consts = sb("consts", [128, 4], F32)
            nc.const_aps.register(F32, EPS_LN, self.consts[:, 0:1])
            nc.const_aps.register(F32, EPS_RMS, self.consts[:, 1:2])
            nc.const_aps.register(F32, -0.5, self.consts[:, 2:3])
            self.s_sy = Sem(ctx, nc, "s_sy")
            self.s_gp = Sem(ctx, nc, "s_gp")
            self.s_pe = Sem(ctx, nc, "s_pe")
            self.s_dve = Sem(ctx, nc, "s_dve")
            self.s_act = Sem(ctx, nc, "s_act")
            self.s_cst = Sem(ctx, nc, "s_cst")
            self.s_tab = Sem(ctx, nc, "s_tab")
            self.s_ld = [Sem(ctx, nc, f"s_ld{i}") for i in range(2)]
            self.s_w = [Sem(ctx, nc, f"s_w{i}") for i in range(2)]
            self.s_st = [Sem(ctx, nc, f"s_st{i}") for i in range(2)]
            self.s_msk = [Sem(ctx, nc, f"s_msk{i}") for i in range(2)]
            self.s_ctx = [Sem(ctx, nc, f"s_ctx{i}") for i in range(2)]
            self.s_wd = [Sem(ctx, nc, f"s_wd{i}") for i in range(4)]
            self.all_dma_sems = ([self.s_sy, self.s_gp, self.s_cst, self.s_tab] + self.s_ld +
                                 self.s_w + self.s_st + self.s_msk + self.s_ctx + self.s_wd)
            self.block = ctx.enter_context(nc.Block())
            block = self.block

            @block.sync
            def _(sy):
                self.s_cst.inc(sy.dma_start(out=self.idf[:], in_=self.id_f32), 16)
                self.s_cst.inc(sy.dma_start(out=self.qkn_sb[:], in_=self.qkn), 16)
            d_const = self.s_cst.n

            @block.vector
            def _(v):
                v.memset(self.ones_col[:], 1.0)
                v.memset(self.ones_row[:], 1.0)
                v.memset(self.consts[:, 0:1], EPS_LN)
                v.memset(self.consts[:, 1:2], EPS_RMS)
                v.memset(self.consts[:, 2:3], -0.5)
                v.wait_ge(self.s_cst.h, d_const)
                v.tensor_copy(self.idb[:], self.idf[:])
                self.s_dve.inc(v.engine_nop())

            with ExitStack() as l1:
                sb1 = lambda name, shape, dt: l1.enter_context(nc.sbuf_tensor(name, list(shape), dt))
                self.PSF = l1.enter_context(nc.psum_tensor("psf", [128, 3584], F32))
                self.PSB = l1.enter_context(nc.psum_tensor("psb", [128, 1024], BF16))
                self.KT = sb1("KT", [128, 2 * L], BF16)
                self.V_sb = sb1("V_sb", [128, NKT * VP], BF16)
                if self.phases >= 1:
                    self.phase_kv(nc, d_const)
                with ExitStack() as lqa:
                    self.qT_sb = lqa.enter_context(
                        nc.sbuf_tensor("qT_sb", [128, 2 * H * TOK], BF16))
                    if self.phases >= 2:
                        self.phase_q(nc)
                    if self.phases >= 3:
                        self.phase_attn(nc)
            if self.phases >= 4:
                with ExitStack() as l2o:
                    self.PS8 = l2o.enter_context(nc.psum_tensor("ps8", [128, 4096], F32))
                    with ExitStack() as l2:
                        sb2 = lambda name, shape, dt: l2.enter_context(nc.sbuf_tensor(name, list(shape), dt))
                        self.ln2T = sb2("ln2T", [128, NDT * TOK], BF16)
                        self.phase_o(nc, sb2)
                        if self.phases >= 5:
                            self.phase_ffn_a(nc)
                    if self.phases >= 6:
                        self.phase_down(nc)
                    if self.phases >= 7:
                        self.phase_epilogue(nc)

            @block.sync
            def _(sy):
                for s in self.all_dma_sems + [self.s_pe, self.s_dve, self.s_act]:
                    s.wait(sy)
        return nc


    def phase_kv(self, nc, d_const):
        block = self.block
        s_sy, s_gp, s_pe, s_dve, s_act = self.s_sy, self.s_gp, self.s_pe, self.s_dve, self.s_act
        bank = lambda i, off=0, n=512: self.PSF[:, 512 * i + off: 512 * i + off + n]
        rows = self.rows
        with ExitStack() as p1:
            sb1 = lambda name, shape, dt: p1.enter_context(nc.sbuf_tensor(name, list(shape), dt))
            cosT_sb = sb1("cosT_sb", [128, L], F32)
            sinT_sb = sb1("sinT_sb", [128, L], F32)
            x_sl = [sb1(f"x_sl{i}", [128, 4096], F32) for i in range(2)]
            xbf = [sb1(f"xbf{i}", [128, 4096], BF16) for i in range(2)]
            xsq = [sb1(f"xsq{i}", [128, 4096], BF16) for i in range(2)]
            ln1_sl = [sb1(f"ln1_sl{i}", [128, 4096], BF16) for i in range(2)]
            k_sb = sb1("k_sb", [128, 512], F32)
            ksq = sb1("ksq", [128, 512], BF16)
            kn_sb = sb1("kn_sb", [128, 512], F32)
            kt0 = sb1("kt0", [128, 256], F32)
            kt1 = sb1("kt1", [128, 256], F32)
            wk_sb = sb1("wk_sb", [128, NDT * 256], BF16)
            wv_sb = sb1("wv_sb", [128, NDT * 256], BF16)
            KT, V_sb = self.KT, self.V_sb
            mk = {}

            @block.gpsimd
            def _(g):
                s_gp.inc(g.dma_start(
                    out=wk_sb[:].rearrange("p (n c) -> p n c", n=NDT),
                    in_=self.wkT.rearrange("(n p) c -> p n c", p=128)), 16)
                s_gp.inc(g.dma_start(
                    out=wv_sb[:].rearrange("p (n c) -> p n c", n=NDT),
                    in_=self.wvT.rearrange("(n p) c -> p n c", p=128)), 16)
            d_wkv = s_gp.n

            @block.sync
            def _(sy):
                self.s_tab.inc(sy.dma_start(out=cosT_sb[:], in_=self.cosT), 16)
                self.s_tab.inc(sy.dma_start(out=sinT_sb[:], in_=self.sinT), 16)
            d_tab = self.s_tab.n

            @block.vector
            def _(v):
                v.memset(V_sb[:], 0.0)
                v.memset(V_sb[:, 256:NKT * VP:VP], 1.0)
                s_dve.inc(v.engine_nop())

            for sl in range(16):
                p = sl % 2
                tok0 = sl * 256

                @block.sync
                def _(sy, sl=sl, p=p, tok0=tok0):
                    if sl >= 2:
                        s_dve.wait(sy, mk[("apply", sl - 2)])
                    self.s_ld[p].inc(sy.dma_start(
                        out=x_sl[p][:].rearrange("p (n c) -> p n c", n=NDT),
                        in_=self.xT_full.rearrange("(n p) l -> p n l", p=128)
                            [:, :, tok0:tok0 + 256]), 16)
                d_x = self.s_ld[p].n

                @block.vector
                def _(v, d_x=d_x, sl=sl, p=p):
                    v.wait_ge(self.s_ld[p].h, d_x)
                    if sl >= 2:
                        s_pe.wait(v, mk[("stat", sl - 2)])
                    v.tensor_copy(xbf[p][:], x_sl[p][:])
                    s_dve.inc(v.engine_nop())
                d_xbf = s_dve.n

                @block.scalar
                def _(s, d_x=d_x, sl=sl, p=p):
                    s.wait_ge(self.s_ld[p].h, d_x)
                    if sl >= 2:
                        s_pe.wait(s, mk[("stat", sl - 2)])
                    s_act.inc(s.activation(xsq[p][:], x_sl[p][:], AF.Square))
                d_xsq = s_act.n

                @block.tensor
                def _(t, d1=d_xbf, d2=d_xsq, sl=sl, p=p):
                    t = _PE(t)
                    t.wait_ge(s_dve.h, d1)
                    t.wait_ge(s_act.h, d2)
                    if sl >= 1:
                        s_dve.wait(t, mk[("rows", sl - 1)])
                    for dt in range(NDT):
                        t.matmul(bank(3, 0, 256)[0:1, :], self.ones_col[:],
                                 xbf[p][:, dt * 256:(dt + 1) * 256],
                                 start=(dt == 0), stop=(dt == NDT - 1))
                    for dt in range(NDT):
                        t.matmul(bank(3, 256, 256)[0:1, :], self.ones_col[:],
                                 xsq[p][:, dt * 256:(dt + 1) * 256],
                                 start=(dt == 0), stop=(dt == NDT - 1))
                    s_pe.inc(t.last)
                mk[("stat", sl)] = d_stat = s_pe.n

                @block.vector
                def _(v, d=d_stat):
                    v.wait_ge(s_pe.h, d)
                    v.tensor_scalar_mul(rows[0:1, 0:256], bank(3, 0, 256)[0:1, :], 1.0 / D)
                    v.tensor_scalar_mul(rows[0:1, 512:768], bank(3, 256, 256)[0:1, :], 1.0 / D)
                    v.tensor_mul(rows[0:1, 1024:1280], rows[0:1, 0:256], rows[0:1, 0:256])
                    v.tensor_sub(rows[0:1, 1024:1280], rows[0:1, 512:768], rows[0:1, 1024:1280])
                    s_dve.inc(v.engine_nop())
                d_var = s_dve.n

                @block.scalar
                def _(s, d=d_var):
                    s.wait_ge(s_dve.h, d)
                    s.activation(rows[0:1, 1280:1536], rows[0:1, 1024:1280], AF.Ln, bias=EPS_LN)
                    s_act.inc(s.activation(rows[0:1, 1536:1792], rows[0:1, 1280:1536],
                                           AF.Exp, scale=-0.5))
                d_rstd = s_act.n

                @block.vector
                def _(v, d=d_rstd):
                    v.wait_ge(s_act.h, d)
                    v.scalar_tensor_tensor(rows[0:1, 2048:2304], rows[0:1, 0:256], -1.0,
                                           rows[0:1, 1536:1792], op0=ALU.mult, op1=ALU.mult)
                    s_dve.inc(v.engine_nop())
                mk[("rows", sl)] = d_rows = s_dve.n

                @block.tensor
                def _(t, d=d_rows, sl=sl):
                    t = _PE(t)
                    t.wait_ge(s_dve.h, d)
                    if sl >= 2:
                        s_dve.wait(t, mk[("apply", sl - 2)])
                    bb = 5 + (sl % 2)
                    t.matmul(bank(bb, 0, 256), self.ones_row[:], rows[0:1, 1536:1792],
                             start=True, stop=True)
                    t.matmul(bank(bb, 256, 256), self.ones_row[:], rows[0:1, 2048:2304],
                             start=True, stop=True)
                    s_pe.inc(t.last)
                d_bc = s_pe.n

                @block.vector
                def _(v, d=d_bc, sl=sl, p=p):
                    v.wait_ge(s_pe.h, d)
                    if sl >= 2:
                        s_pe.wait(v, mk[("vmm", sl - 2, 1)])
                    bb = 5 + (sl % 2)
                    for dt in range(NDT):
                        cs = slice(dt * 256, (dt + 1) * 256)
                        v.tensor_mul(x_sl[p][:, cs], x_sl[p][:, cs], bank(bb, 0, 256))
                        v.tensor_add(ln1_sl[p][:, cs], x_sl[p][:, cs], bank(bb, 256, 256))
                    s_dve.inc(v.engine_nop())
                mk[("apply", sl)] = d_apply = s_dve.n

                @block.tensor
                def _(t, d=d_apply, sl=sl, p=p):
                    t = _PE(t)
                    t.wait_ge(s_dve.h, d)
                    if sl == 0:
                        t.wait_ge(s_gp.h, d_wkv)
                    if sl >= 1:
                        s_dve.wait(t, mk[("kevac", sl - 1)])
                    for half in range(2):
                        for dt in range(NDT):
                            t.matmul(bank(0, half * 256, 256),
                                     wk_sb[:, dt * 256 + half * 128: dt * 256 + half * 128 + 128],
                                     ln1_sl[p][:, dt * 256:(dt + 1) * 256],
                                     start=(dt == 0), stop=(dt == NDT - 1))
                    s_pe.inc(t.last)
                d_kmm = s_pe.n

                @block.tensor
                def _(t, sl=sl, p=p):
                    t = _PE(t)
                    for tt in range(2):
                        if sl >= 1:
                            s_dve.wait(t, mk[("vevac", sl - 1, tt)])
                        for dt in range(NDT):
                            t.matmul(bank(1 + tt, 0, 256),
                                     ln1_sl[p][:, dt * 256 + tt * 128: dt * 256 + tt * 128 + 128],
                                     wv_sb[:, dt * 256:(dt + 1) * 256],
                                     start=(dt == 0), stop=(dt == NDT - 1))
                        s_pe.inc(t.last)
                        mk[("vmm", sl, tt)] = s_pe.n

                for tt in range(2):
                    @block.vector
                    def _(v, sl=sl, tt=tt):
                        s_pe.wait(v, mk[("vmm", sl, tt)])
                        tile = sl * 2 + tt
                        v.tensor_copy(V_sb[:, tile * VP: tile * VP + 256], bank(1 + tt, 0, 256))
                        s_dve.inc(v.engine_nop())
                        mk[("vevac", sl, tt)] = s_dve.n

                @block.vector
                def _(v, d=d_kmm, sl=sl):
                    v.wait_ge(s_pe.h, d)
                    if sl >= 1:
                        s_act.wait(v, mk[("ksq", sl - 1)])
                    v.tensor_copy(k_sb[:], bank(0, 0, 512))
                    s_dve.inc(v.engine_nop())
                mk[("kevac", sl)] = d_kevac = s_dve.n

                @block.scalar
                def _(s, d=d_kevac):
                    s.wait_ge(s_dve.h, d)
                    s_act.inc(s.activation(ksq[:], k_sb[:], AF.Square))
                mk[("ksq", sl)] = d_ksq = s_act.n

                @block.tensor
                def _(t, d=d_ksq, sl=sl):
                    t = _PE(t)
                    t.wait_ge(s_act.h, d)
                    if sl >= 1:
                        s_dve.wait(t, mk[("krows", sl - 1)])
                    for half in range(2):
                        t.matmul(bank(4, 0, 256)[0:1, :], self.ones_col[:],
                                 ksq[:, half * 256:(half + 1) * 256],
                                 start=(half == 0), stop=(half == 1))
                    s_pe.inc(t.last)
                d_kms = s_pe.n

                @block.vector
                def _(v, d=d_kms):
                    v.wait_ge(s_pe.h, d)
                    v.tensor_scalar_mul(rows[0:1, 2560:2816], bank(4, 0, 256)[0:1, :], 1.0 / HD)
                    s_dve.inc(v.engine_nop())
                mk[("krows", sl)] = d_kvar = s_dve.n

                @block.scalar
                def _(s, d=d_kvar):
                    s.wait_ge(s_dve.h, d)
                    s.activation(rows[0:1, 2816:3072], rows[0:1, 2560:2816], AF.Ln, bias=EPS_RMS)
                    s_act.inc(s.activation(rows[0:1, 3072:3328], rows[0:1, 2816:3072],
                                           AF.Exp, scale=-0.5))
                d_krstd = s_act.n

                @block.tensor
                def _(t, d=d_krstd, sl=sl):
                    t = _PE(t)
                    t.wait_ge(s_act.h, d)
                    if sl >= 1:
                        s_dve.wait(t, mk[("kdone", sl - 1)])
                    t.matmul(bank(4, 256, 256), self.ones_row[:], rows[0:1, 3072:3328],
                             start=True, stop=True)
                    s_pe.inc(t.last)
                d_kbc = s_pe.n

                @block.vector
                def _(v, d=d_kbc, tok0=tok0, sl=sl):
                    v.wait_ge(s_pe.h, d)
                    if sl == 0:
                        v.wait_ge(self.s_tab.h, d_tab)
                    for half in range(2):
                        cs = slice(half * 256, (half + 1) * 256)
                        v.tensor_mul(kn_sb[:, cs], k_sb[:, cs], bank(4, 256, 256))
                        v.tensor_scalar_mul(kn_sb[:, cs], kn_sb[:, cs],
                                            self.qkn_sb[:, 2 + half:3 + half])
                    c = cosT_sb[:, tok0:tok0 + 256]
                    s_ = sinT_sb[:, tok0:tok0 + 256]
                    v.tensor_mul(kt0[:], kn_sb[:, 0:256], c)
                    v.tensor_mul(kt1[:], kn_sb[:, 256:512], s_)
                    v.tensor_sub(KT[:, tok0:tok0 + 256], kt0[:], kt1[:])
                    v.tensor_mul(kt0[:], kn_sb[:, 0:256], s_)
                    v.tensor_mul(kt1[:], kn_sb[:, 256:512], c)
                    v.tensor_add(KT[:, L + tok0: L + tok0 + 256], kt0[:], kt1[:])
                    s_dve.inc(v.engine_nop())
                mk[("kdone", sl)] = s_dve.n

            t_KT = self.tap("KT", [128, 2 * L], BF16)
            t_V = self.tap("V", [128, NKT * VP], BF16)
            if t_KT is not None or t_V is not None:
                @block.sync
                def _(sy):
                    s_dve.wait(sy)
                    if t_KT is not None:
                        s_sy.inc(sy.dma_start(out=t_KT, in_=KT[:]), 16)
                    if t_V is not None:
                        s_sy.inc(sy.dma_start(out=t_V, in_=V_sb[:]), 16)
                    sy.wait_ge(s_sy.h, s_sy.n)

    def phase_q(self, nc):
        qcut = 0
        block = self.block
        s_sy, s_gp, s_pe, s_dve, s_act = self.s_sy, self.s_gp, self.s_pe, self.s_dve, self.s_act
        bank = lambda i, off=0, n=512: self.PSF[:, 512 * i + off: 512 * i + off + n]
        rows = self.rows
        self.join()
        if qcut == 1:
            return
        with ExitStack() as p2:
            sb2 = lambda name, shape, dt: p2.enter_context(nc.sbuf_tensor(name, list(shape), dt))
            co_sb = sb2("co_sb", [128, TOK], F32)
            so_sb = sb2("so_sb", [128, TOK], F32)
            ln1o = sb2("ln1o", [128, NDT * 1024], BF16)
            x_sl = sb2("x_sl2", [128, 2048], F32)
            xbf = sb2("xbf2", [128, 2048], BF16)
            xsq = sb2("xsq2", [128, 2048], BF16)
            wq_sb = [sb2(f"wq_sb{i}", [128, NDT * 128], BF16) for i in range(2)]
            qf = [sb2(f"qf{i}", [128, 1024], F32) for i in range(2)]
            rbc = sb2("rbc", [128, 1024], F32)
            mk = {}

            @block.sync
            def _(sy):
                self.s_tab.inc(sy.dma_start(out=co_sb[:], in_=self.cos_own), 16)
                self.s_tab.inc(sy.dma_start(out=so_sb[:], in_=self.sin_own), 16)
            d_tab = self.s_tab.n

            for th in range(2):
                hb = th * 1024
                for sl in range(8):
                    tok0 = th * 1024 + sl * 128

                    @block.sync
                    def _(sy, sl=sl, th=th, tok0=tok0):
                        if (th, sl) != (0, 0):
                            s_dve.wait(sy, max(mk[("apply",)], mk.get(("qrope", 7), 0)))
                        self.s_ld[0].inc(sy.dma_start(
                            out=x_sl[:].rearrange("p (n c) -> p n c", n=NDT),
                            in_=self.xT_own.rearrange("(n p) l -> p n l", p=128)
                                [:, :, tok0:tok0 + 128]), 16)
                    d_x = self.s_ld[0].n

                    @block.vector
                    def _(v, d=d_x, sl=sl, th=th):
                        v.wait_ge(self.s_ld[0].h, d)
                        if (th, sl) != (0, 0):
                            s_pe.wait(v, max(mk[("stat",)], mk.get(("qms",), 0)))
                        v.tensor_copy(xbf[:], x_sl[:])
                        s_dve.inc(v.engine_nop())
                    d_xbf = s_dve.n

                    @block.scalar
                    def _(s, d=d_x, sl=sl, th=th):
                        s.wait_ge(self.s_ld[0].h, d)
                        if (th, sl) != (0, 0):
                            s_pe.wait(s, mk[("stat",)])
                        s_act.inc(s.activation(xsq[:], x_sl[:], AF.Square))
                    d_xsq = s_act.n

                    @block.tensor
                    def _(t, d1=d_xbf, d2=d_xsq, sl=sl, th=th):
                        t = _PE(t)
                        t.wait_ge(s_dve.h, d1)
                        t.wait_ge(s_act.h, d2)
                        if (th, sl) != (0, 0):
                            s_dve.wait(t, mk[("rows",)])
                        for dt in range(NDT):
                            t.matmul(bank(3, 0, 128)[0:1, :], self.ones_col[:],
                                     xbf[:, dt * 128:(dt + 1) * 128],
                                     start=(dt == 0), stop=(dt == NDT - 1))
                        for dt in range(NDT):
                            t.matmul(bank(3, 128, 128)[0:1, :], self.ones_col[:],
                                     xsq[:, dt * 128:(dt + 1) * 128],
                                     start=(dt == 0), stop=(dt == NDT - 1))
                        s_pe.inc(t.last)
                    mk[("stat",)] = d_stat = s_pe.n

                    @block.vector
                    def _(v, d=d_stat):
                        v.wait_ge(s_pe.h, d)
                        v.tensor_scalar_mul(rows[0:1, 0:128], bank(3, 0, 128)[0:1, :], 1.0 / D)
                        v.tensor_scalar_mul(rows[0:1, 512:640], bank(3, 128, 128)[0:1, :], 1.0 / D)
                        v.tensor_mul(rows[0:1, 1024:1152], rows[0:1, 0:128], rows[0:1, 0:128])
                        v.tensor_sub(rows[0:1, 1024:1152], rows[0:1, 512:640], rows[0:1, 1024:1152])
                        s_dve.inc(v.engine_nop())
                    d_var = s_dve.n

                    @block.scalar
                    def _(s, d=d_var):
                        s.wait_ge(s_dve.h, d)
                        s.activation(rows[0:1, 1280:1408], rows[0:1, 1024:1152], AF.Ln, bias=EPS_LN)
                        s_act.inc(s.activation(rows[0:1, 1536:1664], rows[0:1, 1280:1408],
                                               AF.Exp, scale=-0.5))
                    d_rstd = s_act.n

                    @block.vector
                    def _(v, d=d_rstd):
                        v.wait_ge(s_act.h, d)
                        v.scalar_tensor_tensor(rows[0:1, 2048:2176], rows[0:1, 0:128], -1.0,
                                               rows[0:1, 1536:1664], op0=ALU.mult, op1=ALU.mult)
                        s_dve.inc(v.engine_nop())
                    mk[("rows",)] = d_rows = s_dve.n

                    @block.tensor
                    def _(t, d=d_rows, sl=sl, th=th):
                        t = _PE(t)
                        t.wait_ge(s_dve.h, d)
                        s_dve.wait(t, mk.get(("applyprev", sl % 2), 0))
                        bb = 5 + (sl % 2)
                        t.matmul(bank(bb, 0, 128), self.ones_row[:], rows[0:1, 1536:1664],
                                 start=True, stop=True)
                        t.matmul(bank(bb, 128, 128), self.ones_row[:], rows[0:1, 2048:2176],
                                 start=True, stop=True)
                        s_pe.inc(t.last)
                    d_bc = s_pe.n

                    @block.vector
                    def _(v, d=d_bc, sl=sl, th=th):
                        v.wait_ge(s_pe.h, d)
                        if th == 1:
                            s_pe.wait(v, mk.get(("qmmw", 1), 0))
                        bb = 5 + (sl % 2)
                        for dt in range(NDT):
                            cs = slice(dt * 128, (dt + 1) * 128)
                            ds = slice(dt * 1024 + sl * 128, dt * 1024 + sl * 128 + 128)
                            v.tensor_mul(x_sl[:, cs], x_sl[:, cs], bank(bb, 0, 128))
                            v.tensor_add(ln1o[:, ds], x_sl[:, cs], bank(bb, 128, 128))
                        s_dve.inc(v.engine_nop())
                    mk[("apply",)] = mk[("applyprev", sl % 2)] = s_dve.n
                d_lnown = s_dve.n

                if qcut == 2:
                    continue
                for h in range(H):
                    for half in range(2):
                        ft = 2 * h + half
                        wp = ft % 2

                        @block.gpsimd
                        def _(g, ft=ft, wp=wp, th=th):
                            s_pe.wait(g, mk.get(("qmmw", wp), 0))
                            self.s_w[wp].inc(g.dma_start(
                                out=wq_sb[wp][:].rearrange("p (n c) -> p n c", n=NDT),
                                in_=self.wqT.rearrange("(n p) c -> p n c", p=128)
                                    [:, :, ft * 128:(ft + 1) * 128]), 16)
                        d_w = self.s_w[wp].n

                        @block.tensor
                        def _(t, d=d_w, half=half, wp=wp, th=th, ft=ft):
                            t = _PE(t)
                            t.wait_ge(self.s_w[wp].h, d)
                            if ft == 0:
                                t.wait_ge(s_dve.h, d_lnown)
                            s_dve.wait(t, mk.get(("qevac", half), 0))
                            for dt in range(NDT):
                                for nn in range(2):
                                    t.matmul(bank(half * 2 + nn, 0, 512),
                                             wq_sb[wp][:, dt * 128:(dt + 1) * 128],
                                             ln1o[:, dt * 1024 + nn * 512: dt * 1024 + (nn + 1) * 512],
                                             start=(dt == 0), stop=(dt == NDT - 1))
                            s_pe.inc(t.last)
                        mk[("qmmw", wp)] = d_qmm = s_pe.n

                        @block.vector
                        def _(v, d=d_qmm, half=half):
                            v.wait_ge(s_pe.h, d)
                            v.tensor_copy(qf[half][:, 0:512], bank(half * 2, 0, 512))
                            v.tensor_copy(qf[half][:, 512:1024], bank(half * 2 + 1, 0, 512))
                            s_dve.inc(v.engine_nop())
                        mk[("qevac", half)] = s_dve.n
                    d_qevac = s_dve.n
                    if qcut == 3:
                        continue

                    @block.scalar
                    def _(s, d=d_qevac, h=h):
                        s.wait_ge(s_dve.h, d)
                        s_pe.wait(s, mk.get(("qms",), 0))
                        s.activation(xbf[:, 0:1024], qf[0][:], AF.Square)
                        s_act.inc(s.activation(xbf[:, 1024:2048], qf[1][:], AF.Square))
                    mk[("qsqm",)] = d_qsq = s_act.n

                    @block.tensor
                    def _(t, d=d_qsq, h=h):
                        t = _PE(t)
                        t.wait_ge(s_act.h, d)
                        s_dve.wait(t, mk.get(("qvar",), 0))
                        for cc in range(2):
                            for half in range(2):
                                t.matmul(bank(4 + cc, 0, 512)[0:1, :], self.ones_col[:],
                                         xbf[:, half * 1024 + cc * 512:
                                             half * 1024 + (cc + 1) * 512],
                                         start=(half == 0), stop=(half == 1))
                        s_pe.inc(t.last)
                    mk[("qms",)] = d_qms = s_pe.n

                    @block.vector
                    def _(v, d=d_qms):
                        v.wait_ge(s_pe.h, d)
                        v.tensor_scalar_mul(rows[0:1, 0:512], bank(4, 0, 512)[0:1, :], 1.0 / HD)
                        v.tensor_scalar_mul(rows[0:1, 512:1024], bank(5, 0, 512)[0:1, :], 1.0 / HD)
                        s_dve.inc(v.engine_nop())
                    mk[("qvar",)] = d_qvar = s_dve.n

                    @block.scalar
                    def _(s, d=d_qvar):
                        s.wait_ge(s_dve.h, d)
                        s.activation(rows[0:1, 1024:2048], rows[0:1, 0:1024], AF.Ln, bias=EPS_RMS)
                        s_act.inc(s.activation(rows[0:1, 2048:3072], rows[0:1, 1024:2048],
                                               AF.Exp, scale=-0.5))
                    d_qrstd = s_act.n

                    for cc in range(2):
                        @block.tensor
                        def _(t, cc=cc, d=d_qrstd):
                            t = _PE(t)
                            t.wait_ge(s_act.h, d)
                            s_dve.wait(t, mk.get(("rbcc", 0), 0))
                            s_dve.wait(t, mk.get(("rbcc", 1), 0))
                            t.matmul(bank(6, 0, 512), self.ones_row[:],
                                     rows[0:1, 2048 + cc * 512: 2048 + (cc + 1) * 512],
                                     start=True, stop=True)
                            s_pe.inc(t.last)
                        d_bcmm = s_pe.n

                        @block.vector
                        def _(v, cc=cc, d=d_bcmm):
                            v.wait_ge(s_pe.h, d)
                            v.tensor_copy(rbc[:, cc * 512:(cc + 1) * 512], bank(6, 0, 512))
                            s_dve.inc(v.engine_nop())
                            mk[("rbcc", cc)] = s_dve.n

                    if qcut == 4:
                        continue

                    @block.vector
                    def _(v, h=h, th=th, hb=hb, dq=d_qsq):
                        v.wait_ge(s_act.h, dq)
                        for half in range(2):
                            v.tensor_mul(qf[half][:], qf[half][:], rbc[:])
                            v.tensor_scalar(qf[half][:], qf[half][:],
                                            self.qkn_sb[:, half:half + 1], 1.0 / 16.0,
                                            op0=ALU.mult, op1=ALU.mult)
                        ta = x_sl[:, 0:1024]
                        tb = x_sl[:, 1024:2048]
                        co = co_sb[:, hb:hb + 1024]
                        so = so_sb[:, hb:hb + 1024]
                        o1 = self.qT_sb[:, (2 * h) * TOK + hb: (2 * h) * TOK + hb + 1024]
                        o2 = self.qT_sb[:, (2 * h + 1) * TOK + hb: (2 * h + 1) * TOK + hb + 1024]
                        v.tensor_mul(ta, qf[0][:], co)
                        v.tensor_mul(tb, qf[1][:], so)
                        v.tensor_sub(o1, ta, tb)
                        v.tensor_mul(ta, qf[0][:], so)
                        v.tensor_mul(tb, qf[1][:], co)
                        v.tensor_add(o2, ta, tb)
                        s_dve.inc(v.engine_nop())
                    mk[("qrope", h)] = s_dve.n
                if th == 0:
                    @block.vector
                    def _(v):
                        v.wait_ge(self.s_tab.h, d_tab)
                        s_dve.inc(v.engine_nop())

            t_qT = self.tap("qT", [128, 2 * H * TOK], BF16)
            if t_qT is not None:
                @block.sync
                def _(sy):
                    s_dve.wait(sy)
                    s_sy.inc(sy.dma_start(out=t_qT, in_=self.qT_sb[:]), 16)
                    sy.wait_ge(s_sy.h, s_sy.n)

    def join(self):
        block = self.block
        s_sy, s_gp, s_pe, s_dve, s_act = self.s_sy, self.s_gp, self.s_pe, self.s_dve, self.s_act

        dmas = self.all_dma_sems

        @block.tensor
        def _(t):
            for s in dmas + [s_dve, s_act]:
                s.wait(t)
            s_pe.inc(t.nop())

        @block.vector
        def _(v):
            for s in dmas + [s_pe, s_act]:
                s.wait(v)
            s_dve.inc(v.engine_nop())

        @block.scalar
        def _(s_):
            for s in dmas + [s_pe, s_dve]:
                s.wait(s_)
            s_act.inc(s_.copy(self.ones_row[0:1, 0:1], self.ones_row[0:1, 0:1]))

        @block.sync
        def _(sy):
            for s in dmas + [s_pe, s_dve, s_act]:
                s.wait(sy)

        @block.gpsimd
        def _(g):
            for s in dmas + [s_pe, s_dve, s_act]:
                s.wait(g)

    def phase_attn(self, nc):
        block = self.block
        s_sy, s_pe, s_dve, s_act = self.s_sy, self.s_pe, self.s_dve, self.s_act
        self.join()
        with ExitStack() as p3:
            sb3 = lambda name, shape, dt: p3.enter_context(nc.sbuf_tensor(name, list(shape), dt))
            mask_all = sb3("mask_all", [128, NT * 256], BF16)
            w_sb = [sb3(f"w_sb{i}", [128, 3072], BF16) for i in range(2)]
            wT_sb = [sb3(f"wT_sb{i}", [128, 3072], BF16) for i in range(2)]
            ctx_j = sb3("ctx_j", [128, 2048], BF16)
            ctxT_j = [sb3(f"ctxT_j{i}", [128, 2048], BF16) for i in range(2)]
            acc = sb3("acc01", [128, 16 * 264], F32)
            mns = sb3("mns", [128, 96], F32)
            rc_sb = sb3("rc_sb", [128, 8], F32)
            KT, V_sb, qT_sb = self.KT, self.V_sb, self.qT_sb
            qcol = lambda h, half, j: qT_sb[:, ((2 * h + half) * NT + j) * 128:
                                            ((2 * h + half) * NT + j) * 128 + 128]
            SC = self.PSF[:, 0:2560]
            CB = self.PSF[:, 2560:2817]
            tslot = [self.PSB[:, 0:512], self.PSF[:, 3072:3584].bitcast(BF16)]
            mxc = self.mx_cols
            mk = {}
            psb = [0]

            @block.sync
            def _(sy):
                self.s_tab.inc(sy.dma_start(
                    out=mask_all[:].rearrange("p (j c) -> p j c", j=NT),
                    in_=self.masks.rearrange("j p c -> p j c")), 16)
            d_mask = self.s_tab.n

            units = []
            for j in range(self.jcap):
                nkt = 2 * (j + 1)
                wins = [(0, nkt)] if nkt <= 20 else [(0, 16), (16, nkt - 16)]
                for h in range(H):
                    for wi, (tb, tn) in enumerate(wins):
                        units.append((j, h, wi, tb, tn, len(wins)))

            def mcol(j, wi, h):
                return (j % 2) * 48 + wi * 8 + h

            def emit_scores(u):
                j, h, wi, tb, tn, nw = units[u]
                W = tn * 128
                chunks = [(cp, min(512, W - cp)) for cp in range(0, W, 512)]
                last_win = (wi == nw - 1)

                @block.tensor
                def _(t, j=j, h=h, tb=tb, chunks=chunks, u=u):
                    t = _PE(t)
                    if u > 0:
                        t.wait_ge(s_act.h, mk[("exp", u - 1)])
                    for half in range(2):
                        for ci, (cp, cl) in enumerate(chunks):
                            mm = t.matmul(
                                SC[:, cp:cp + cl], qcol(h, half, j),
                                KT[:, half * L + tb * 128 + cp:
                                   half * L + tb * 128 + cp + cl],
                                start=(half == 0), stop=(half == 1),
                                skip_group_check=True)
                            if half == 1:
                                s_pe.inc(mm)
                                mk[("sc", u, ci)] = s_pe.n

                @block.vector
                def _(v, j=j, h=h, wi=wi, chunks=chunks, u=u, nch=len(chunks)):
                    r = None
                    for ci, (cp, cl) in enumerate(chunks):
                        v.wait_ge(s_pe.h, mk[("sc", u, ci)])
                        r = v.reduce_max(mxc[:, ci:ci + 1], SC[:, cp:cp + cl], axis=AXX)
                    s_dve.inc(r)
                    v.wait_ge(s_dve.h, s_dve.n)
                    c = mcol(j, wi, h)
                    s_dve.inc(v.tensor_reduce(mns[:, c:c + 1], mxc[:, 0:nch],
                                              axis=AXX, op=ALU.max, negate=True))
                mk[("mx", u)] = s_dve.n

                @block.scalar
                def _(s, u=u, W=W, c=mcol(j, wi, h)):
                    s.wait_ge(s_dve.h, mk[("mx", u)])
                    if u >= 2:
                        s_pe.wait(s, mk.get(("tp_last", u - 2), 0))
                    s_act.inc(s.activation(w_sb[u % 2][:, 0:W], SC[:, 0:W], AF.Exp,
                                           bias=mns[:, c:c + 1]))
                mk[("exp", u)] = s_act.n

                if last_win:
                    @block.vector
                    def _(v, j=j, u=u, W=W):
                        s_act.wait(v, mk[("exp", u)])
                        if u == 0:
                            v.wait_ge(self.s_tab.h, d_mask)
                        s_dve.inc(v.tensor_mul(
                            w_sb[u % 2][:, W - 256:W], w_sb[u % 2][:, W - 256:W],
                            mask_all[:, j * 256:(j + 1) * 256]))
                    mk[("wmask", u)] = s_dve.n
                else:
                    mk[("wmask", u)] = None

            def emit_work(v_):
                j, h, wi, tb, tn, nw = units[v_]
                p = v_ % 2
                ngr = (tn + 3) // 4

                def emit_tp(g):
                    t0g, t1g = g * 4, min(g * 4 + 4, tn)
                    slot = psb[0] % 2
                    psb[0] += 1

                    @block.tensor
                    def _(t, v_=v_, g=g, t0g=t0g, t1g=t1g, slot=slot, p=p, ngr=ngr):
                        t = _PE(t)
                        if mk.get(("wmask", v_)) is not None:
                            s_dve.wait(t, mk[("wmask", v_)])
                        else:
                            t.wait_ge(s_act.h, mk[("exp", v_)])
                        s_dve.wait(t, mk.get(("psb", slot), 0))
                        for ti in range(t0g, t1g):
                            t.transpose(
                                tslot[slot][:, (ti - t0g) * 128: (ti - t0g) * 128 + 128],
                                w_sb[p][:, ti * 128: ti * 128 + 128], self.idb[:])
                        s_pe.inc(t.last)
                    mk[("tp", v_, g)] = s_pe.n
                    if g == ngr - 1:
                        mk[("tp_last", v_)] = s_pe.n

                    @block.vector
                    def _(v, v_=v_, g=g, t0g=t0g, t1g=t1g, slot=slot, p=p):
                        v.wait_ge(s_pe.h, mk[("tp", v_, g)])
                        if g == 0 and v_ >= 2:
                            s_pe.wait(v, mk[("ctxmm", v_ - 2)])
                        nt_ = t1g - t0g
                        s_dve.inc(v.tensor_copy(
                            wT_sb[p][:, t0g * 128: t0g * 128 + nt_ * 128],
                            tslot[slot][:, 0: nt_ * 128]))
                    mk[("psb", slot)] = s_dve.n
                    mk[("ev", v_, g)] = s_dve.n

                def emit_ctx(g):
                    t0g, t1g = g * 4, min(g * 4 + 4, tn)

                    @block.tensor
                    def _(t, v_=v_, g=g, t0g=t0g, t1g=t1g, tb=tb, p=p, ngr=ngr):
                        t = _PE(t)
                        t.wait_ge(s_dve.h, mk[("ev", v_, g)])
                        if g == 0 and v_ >= 1:
                            s_dve.wait(t, mk[("cbfree", v_ - 1)])
                        for ti in range(t0g, t1g):
                            t.matmul(CB, wT_sb[p][:, ti * 128: ti * 128 + 128],
                                     V_sb[:, (tb + ti) * VP: (tb + ti) * VP + 257],
                                     start=(g == 0 and ti == t0g),
                                     stop=(g == ngr - 1 and ti == t1g - 1))
                        if g == ngr - 1:
                            s_pe.inc(t.last)

                for g in range(ngr):
                    emit_tp(g)
                    if g >= 1:
                        emit_ctx(g - 1)
                emit_ctx(ngr - 1)
                mk[("ctxmm", v_)] = s_pe.n

                @block.vector
                def _(v, v_=v_, j=j, h=h, wi=wi, nw=nw):
                    v.wait_ge(s_pe.h, mk[("ctxmm", v_)])
                    if nw == 1:
                        if h == 0 and j > 0:
                            s_pe.wait(v, mk[("ctxT", j - 1)])
                        s_dve.inc(v.reciprocal(rc_sb[:, h:h + 1], CB[:, 256:257]))
                        v.wait_ge(s_dve.h, s_dve.n)
                        s_dve.inc(v.tensor_scalar(
                            ctx_j[:, h * 256:(h + 1) * 256], CB[:, 0:256],
                            rc_sb[:, h:h + 1], None, op0=ALU.mult))
                    else:
                        s_dve.inc(v.tensor_copy(
                            acc[:, (wi * 8 + h) * 264: (wi * 8 + h) * 264 + 257], CB))
                mk[("cbfree", v_)] = s_dve.n

            def emit_tail(j):
                nkt = 2 * (j + 1)
                base = (j % 2) * 48
                if nkt > 20:
                    @block.vector
                    def _(v, base=base):
                        last = None
                        for h2 in range(H):
                            last = v.tensor_scalar(
                                mns[:, base + 16 + h2: base + 17 + h2],
                                mns[:, base + 8 + h2: base + 9 + h2],
                                mns[:, base + h2: base + 1 + h2], 0.0,
                                op0=ALU.subtract, op1=ALU.min)
                            last = v.tensor_scalar(
                                mns[:, base + 24 + h2: base + 25 + h2],
                                mns[:, base + h2: base + 1 + h2],
                                mns[:, base + 8 + h2: base + 9 + h2], 0.0,
                                op0=ALU.subtract, op1=ALU.min)
                        s_dve.inc(last)
                    d_dp = s_dve.n

                    @block.scalar
                    def _(s, d=d_dp, base=base):
                        s.wait_ge(s_dve.h, d)
                        s_act.inc(s.activation(mns[:, base + 32:base + 48],
                                               mns[:, base + 16:base + 32], AF.Exp))
                    d_f = s_act.n

                    @block.vector
                    def _(v, j=j, d=d_f, base=base):
                        v.wait_ge(s_act.h, d)
                        if j > 0:
                            s_pe.wait(v, mk[("ctxT", j - 1)])
                        for h2 in range(H):
                            A0 = acc[:, h2 * 264: h2 * 264 + 257]
                            v.tensor_scalar(A0, A0, mns[:, base + 32 + h2: base + 33 + h2],
                                            None, op0=ALU.mult)
                        for h2 in range(H):
                            A0 = acc[:, h2 * 264: h2 * 264 + 257]
                            A1 = acc[:, (8 + h2) * 264: (8 + h2) * 264 + 257]
                            v.scalar_tensor_tensor(A0, A1,
                                                   mns[:, base + 40 + h2: base + 41 + h2],
                                                   A0, op0=ALU.mult, op1=ALU.add)
                        last = None
                        for h2 in range(H):
                            last = v.reciprocal(rc_sb[:, h2:h2 + 1],
                                                acc[:, h2 * 264 + 256: h2 * 264 + 257])
                        s_dve.inc(last)
                        v.wait_ge(s_dve.h, s_dve.n)
                        last = None
                        for h2 in range(H):
                            last = v.tensor_scalar(ctx_j[:, h2 * 256:(h2 + 1) * 256],
                                                   acc[:, h2 * 264: h2 * 264 + 256],
                                                   rc_sb[:, h2:h2 + 1], None, op0=ALU.mult)
                        s_dve.inc(last)
                mk[("ctxj", j)] = s_dve.n

                jp = j % 2
                for fb in range(0, 16, 4):
                    slot = psb[0] % 2
                    psb[0] += 1

                    @block.tensor
                    def _(t, j=j, fb=fb, slot=slot):
                        t = _PE(t)
                        s_dve.wait(t, mk[("ctxj", j)])
                        s_dve.wait(t, mk.get(("psb", slot), 0))
                        for i in range(4):
                            t.transpose(
                                tslot[slot][:, i * 128: i * 128 + 128],
                                ctx_j[:, (fb + i) * 128:(fb + i) * 128 + 128],
                                self.idb[:])
                        s_pe.inc(t.last)
                    d_tp = s_pe.n
                    if fb == 12:
                        mk[("ctxT", j)] = d_tp

                    @block.vector
                    def _(v, d=d_tp, fb=fb, j=j, jp=jp, slot=slot):
                        v.wait_ge(s_pe.h, d)
                        if fb == 0 and j >= 2 and ("ctxdma", j - 2) in mk:
                            cjp, cval = mk[("ctxdma", j - 2)]
                            v.wait_ge(self.s_ctx[cjp].h, cval)
                        s_dve.inc(v.tensor_copy(
                            ctxT_j[jp][:, fb * 128: fb * 128 + 512],
                            tslot[slot][:, 0:512]))
                    mk[("psb", slot)] = s_dve.n
                d_ctxT = s_dve.n

                if getattr(self, "acap", 4) >= 4:
                    @block.sync
                    def _(sy, d=d_ctxT, j=j, jp=jp):
                        sy.wait_ge(s_dve.h, d)
                        self.s_ctx[jp].inc(sy.dma_start(
                            out=self.ctxT_sc.rearrange("(ft p) q -> p ft q", p=128)
                                [:, :, j * 128:(j + 1) * 128],
                            in_=ctxT_j[jp][:].rearrange("p (ft q) -> p ft q", ft=16)), 16)
                    mk[("ctxdma", j)] = (jp, self.s_ctx[jp].n)

            acap = getattr(self, "acap", 4)
            for u in range(len(units)):
                emit_scores(u)
                if acap >= 2 and u >= 1:
                    emit_work(u - 1)
                    if acap >= 3 and units[u - 1][0] != units[u][0]:
                        emit_tail(units[u - 1][0])
            if acap >= 2:
                emit_work(len(units) - 1)
            if acap >= 3:
                emit_tail(units[-1][0])

            t_ctxT = self.tap("ctxT", [D, TOK], BF16)
            if t_ctxT is not None:
                @block.sync
                def _(sy):
                    for s in self.s_ctx:
                        s.wait(sy)
                    s_sy.inc(sy.dma_start(out=t_ctxT, in_=self.ctxT_sc), 16)
                    sy.wait_ge(s_sy.h, s_sy.n)

    def phase_o(self, nc, sb2):
        block = self.block
        s_sy, s_gp, s_pe, s_dve, s_act = self.s_sy, self.s_gp, self.s_pe, self.s_dve, self.s_act
        PS8 = self.PS8
        rows = self.rows
        self.join()
        with ExitStack() as p4:
            sb4 = lambda name, shape, dt: p4.enter_context(nc.sbuf_tensor(name, list(shape), dt))
            ctxT_sb = sb4("ctxT_sb", [128, NDT * TOK], BF16)
            wo_sb = [sb4(f"wo_sb{i}", [128, 2048], BF16) for i in range(2)]
            rdA = [sb4(f"rdA{i}", [128, 2048], F32) for i in range(2)]
            rdB = [sb4("rdB0", [128, 2048], F32)] * 2
            bfc = sb4("bfc", [128, 2048], BF16)
            sqc = sb4("sqc", [128, 2048], BF16)
            bcA = sb4("bcA", [128, 2048], F32)
            bcB = sb4("bcB", [128, 2048], F32)
            mk = {}

            @block.sync
            def _(sy):
                s_sy.inc(sy.dma_start(
                    out=ctxT_sb[:].rearrange("p (ft q) -> p ft q", ft=NDT),
                    in_=self.ctxT_sc.rearrange("(ft p) q -> p ft q", p=128)), 16)
            d_ctxld = s_sy.n
            mk[("sy_only",)] = True

            for dt in range(NDT):
                pp = dt % 2

                @block.gpsimd
                def _(g, dt=dt, pp=pp):
                    if dt >= 2:
                        s_pe.wait(g, mk[("omm", dt - 2)])
                    self.s_w[pp].inc(g.dma_start(
                        out=wo_sb[pp][:].rearrange("p (n c) -> p n c", n=NDT),
                        in_=self.woT.rearrange("(n p) c -> p n c", p=128)
                            [:, :, dt * 128:(dt + 1) * 128]), 16)
                d_w = self.s_w[pp].n

                @block.tensor
                def _(t, d=d_w, dt=dt, pp=pp):
                    t = _PE(t)
                    t.wait_ge(self.s_w[pp].h, d)
                    if dt == 0:
                        t.wait_ge(s_sy.h, d_ctxld)
                    if dt >= 2:
                        s_dve.wait(t, mk[("aevac", dt - 2)])
                    for ft in range(NDT):
                        for nn in range(4):
                            bn = nn ^ 2
                            t.matmul(PS8[:, pp * 2048 + bn * 512: pp * 2048 + (bn + 1) * 512],
                                     wo_sb[pp][:, ft * 128:(ft + 1) * 128],
                                     ctxT_sb[:, ft * TOK + nn * 512: ft * TOK + (nn + 1) * 512],
                                     start=(ft == 0), stop=(ft == NDT - 1))
                    s_pe.inc(t.last)
                mk[("omm", dt)] = d_omm = s_pe.n

                @block.vector
                def _(v, d=d_omm, dt=dt, pp=pp):
                    v.wait_ge(s_pe.h, d)
                    if dt >= 2:
                        v.wait_ge(self.s_st[pp].h, mk[("adma", dt - 2)])
                    for nn in range(4):
                        bn = nn ^ 2
                        v.tensor_copy(rdA[pp][:, nn * 512:(nn + 1) * 512],
                                      PS8[:, pp * 2048 + bn * 512: pp * 2048 + (bn + 1) * 512])
                    s_dve.inc(v.engine_nop())
                mk[("aevac", dt)] = d_aev = s_dve.n

                @block.sync
                def _(sy, d=d_aev, dt=dt, pp=pp):
                    sy.wait_ge(s_dve.h, d)
                    self.s_st[pp].inc(sy.dma_start(
                        out=self.a_sc.rearrange("(n p) q -> p n q", p=128)[:, dt, :],
                        in_=rdA[pp][:]), 16)
                mk[("adma", dt)] = self.s_st[pp].n

            self._stat_pass(nc, self.a_sc, rdA, bfc, sqc, mk, "a")
            self._rows_and_bcast(nc, bcA, bcB, mk, "a")

            for dt in range(NDT):
                pp = dt % 2

                @block.sync
                def _(sy, dt=dt, pp=pp, g1=self.s_dve.n, g2=self.s_act.n):
                    if dt >= 1:
                        s_dve.wait(sy, mk[("x1done", dt - 1)])
                    if dt < 2:
                        sy.wait_ge(s_dve.h, g1)
                        sy.wait_ge(s_act.h, g2)
                    self.s_ld[pp].inc(sy.dma_start(
                        out=rdA[pp][:], in_=self.a_sc.rearrange("(n p) q -> p n q", p=128)[:, dt, :]), 16)
                    self.s_ld[pp].inc(sy.dma_start(
                        out=rdB[pp][:], in_=self.xT_own.rearrange("(n p) q -> p n q", p=128)[:, dt, :]), 16)
                d_ld = self.s_ld[pp].n

                @block.vector
                def _(v, d=d_ld, dt=dt, pp=pp):
                    v.wait_ge(self.s_ld[pp].h, d)
                    if dt >= 1:
                        s_pe.wait(v, mk[("x1stat", dt - 1)])
                    if dt >= 2:
                        v.wait_ge(self.s_st[pp].h, mk[("x1dma", dt - 2)])
                    for cc in range(4):
                        cs = slice(cc * 512, (cc + 1) * 512)
                        v.tensor_mul(rdA[pp][:, cs], rdA[pp][:, cs], bcA[:, cs])
                        v.tensor_add(rdA[pp][:, cs], rdA[pp][:, cs], bcB[:, cs])
                        v.tensor_add(rdA[pp][:, cs], rdA[pp][:, cs], rdB[pp][:, cs])
                    v.tensor_copy(bfc[:], rdA[pp][:])
                    s_dve.inc(v.engine_nop())
                mk[("x1done", dt)] = d_x1 = s_dve.n

                @block.scalar
                def _(s, d=d_x1, dt=dt, pp=pp):
                    s.wait_ge(s_dve.h, d)
                    s_act.inc(s.activation(sqc[:], rdA[pp][:], AF.Square))
                d_sq = s_act.n

                @block.tensor
                def _(t, d1=d_x1, d2=d_sq, dt=dt):
                    t = _PE(t)
                    t.wait_ge(s_dve.h, d1)
                    t.wait_ge(s_act.h, d2)
                    for cc in range(4):
                        t.matmul(PS8[0:1, cc * 512:(cc + 1) * 512], self.ones_col[:],
                                 bfc[:, cc * 512:(cc + 1) * 512],
                                 start=(dt == 0), stop=(dt == NDT - 1))
                        t.matmul(PS8[0:1, 2048 + cc * 512: 2048 + (cc + 1) * 512], self.ones_col[:],
                                 sqc[:, cc * 512:(cc + 1) * 512],
                                 start=(dt == 0), stop=(dt == NDT - 1))
                    s_pe.inc(t.last)
                mk[("x1stat", dt)] = s_pe.n

                @block.sync
                def _(sy, d=d_x1, dt=dt, pp=pp):
                    sy.wait_ge(s_dve.h, d)
                    self.s_st[pp].inc(sy.dma_start(
                        out=self.x1_sc.rearrange("(n p) q -> p n q", p=128)[:, dt, :],
                        in_=rdA[pp][:]), 16)
                mk[("x1dma", dt)] = self.s_st[pp].n

            self._rows_from_ps8(nc, mk, "x1")
            self._bcast_from_rows(nc, bcA, bcB, mk, "x1")

            for dt in range(NDT):
                pp = dt % 2

                @block.sync
                def _(sy, dt=dt, pp=pp, g1=self.s_dve.n, g2=self.s_act.n):
                    if dt >= 2:
                        s_dve.wait(sy, mk[("ln2ap", dt - 2)])
                    else:
                        sy.wait_ge(s_dve.h, g1)
                        sy.wait_ge(s_act.h, g2)
                    self.s_ld[pp].inc(sy.dma_start(
                        out=rdA[pp][:], in_=self.x1_sc.rearrange("(n p) q -> p n q", p=128)[:, dt, :]), 16)
                d_ld = self.s_ld[pp].n

                @block.vector
                def _(v, d=d_ld, dt=dt, pp=pp):
                    v.wait_ge(self.s_ld[pp].h, d)
                    for cc in range(4):
                        cs = slice(cc * 512, (cc + 1) * 512)
                        v.tensor_mul(rdA[pp][:, cs], rdA[pp][:, cs], bcA[:, cs])
                        v.tensor_add(self.ln2T[:, dt * TOK + cc * 512: dt * TOK + (cc + 1) * 512],
                                     rdA[pp][:, cs], bcB[:, cs])
                    s_dve.inc(v.engine_nop())
                mk[("ln2ap", dt)] = s_dve.n

            t_av = self.tap("av", [D, TOK])
            if t_av is not None:
                for dt in range(NDT):
                    @block.sync
                    def _(sy, dt=dt):
                        s_dve.wait(sy)
                        for s in self.s_st + self.s_ld:
                            s.wait(sy)
                        self.s_ld[0].inc(sy.dma_start(
                            out=rdA[0][:], in_=self.a_sc.rearrange("(n p) q -> p n q", p=128)[:, dt, :]), 16)
                        sy.wait_ge(self.s_ld[0].h, self.s_ld[0].n)
                        self.s_ld[0].inc(sy.dma_start(
                            out=t_av.rearrange("(n p) q -> p n q", p=128)[:, dt, :], in_=rdA[0][:]), 16)
                        sy.wait_ge(self.s_ld[0].h, self.s_ld[0].n)
            t_x1v = self.tap("x1v", [D, TOK])
            if t_x1v is not None:
                for dt in range(NDT):
                    @block.sync
                    def _(sy, dt=dt):
                        s_dve.wait(sy)
                        for s in self.s_st + self.s_ld:
                            s.wait(sy)
                        self.s_ld[0].inc(sy.dma_start(
                            out=rdA[0][:], in_=self.x1_sc.rearrange("(n p) q -> p n q", p=128)[:, dt, :]), 16)
                        sy.wait_ge(self.s_ld[0].h, self.s_ld[0].n)
                        self.s_ld[0].inc(sy.dma_start(
                            out=t_x1v.rearrange("(n p) q -> p n q", p=128)[:, dt, :], in_=rdA[0][:]), 16)
                        sy.wait_ge(self.s_ld[0].h, self.s_ld[0].n)
            t_x1 = self.tap("x1", [D, TOK])
            t_ln2 = self.tap("ln2T", [128, NDT * TOK], BF16)
            if t_x1 is not None or t_ln2 is not None:
                @block.sync
                def _(sy):
                    s_dve.wait(sy)
                    for s in self.s_st:
                        s.wait(sy)
                    if t_x1 is not None:
                        s_sy.inc(sy.dma_start(out=t_x1, in_=self.x1_sc), 16)
                    if t_ln2 is not None:
                        s_sy.inc(sy.dma_start(out=t_ln2, in_=self.ln2T[:]), 16)
                    sy.wait_ge(s_sy.h, s_sy.n)

    def _stat_pass(self, nc, src_sc, rd, bfc, sqc, mk, tag):
        block = self.block
        s_sy, s_pe, s_dve, s_act = self.s_sy, self.s_pe, self.s_dve, self.s_act
        PS8 = self.PS8
        for dt in range(NDT):
            pp = dt % 2

            @block.sync
            def _(sy, dt=dt, pp=pp, g1=self.s_dve.n, g2=self.s_act.n):
                if dt >= 2:
                    s_act.wait(sy, mk[(tag + "sq", dt - 2)])
                    s_dve.wait(sy, mk[(tag + "bf", dt - 2)])
                else:
                    sy.wait_ge(s_dve.h, g1)
                    sy.wait_ge(s_act.h, g2)
                self.s_ld[pp].inc(sy.dma_start(
                    out=rd[pp][:], in_=src_sc.rearrange("(n p) q -> p n q", p=128)[:, dt, :]), 16)
            d_ld = self.s_ld[pp].n

            @block.vector
            def _(v, d=d_ld, dt=dt, pp=pp):
                v.wait_ge(self.s_ld[pp].h, d)
                if dt >= 1:
                    s_pe.wait(v, mk[(tag + "stat", dt - 1)])
                v.tensor_copy(bfc[:], rd[pp][:])
                s_dve.inc(v.engine_nop())
            mk[(tag + "bf", dt)] = d_bf = s_dve.n

            @block.scalar
            def _(s, d=d_ld, dt=dt, pp=pp):
                s.wait_ge(self.s_ld[pp].h, d)
                if dt >= 1:
                    s_pe.wait(s, mk[(tag + "stat", dt - 1)])
                s_act.inc(s.activation(sqc[:], rd[pp][:], AF.Square))
            mk[(tag + "sq", dt)] = d_sq = s_act.n

            @block.tensor
            def _(t, d1=d_bf, d2=d_sq, dt=dt):
                t = _PE(t)
                t.wait_ge(s_dve.h, d1)
                t.wait_ge(s_act.h, d2)
                for cc in range(4):
                    t.matmul(PS8[0:1, cc * 512:(cc + 1) * 512], self.ones_col[:],
                             bfc[:, cc * 512:(cc + 1) * 512],
                             start=(dt == 0), stop=(dt == NDT - 1))
                    t.matmul(PS8[0:1, 2048 + cc * 512: 2048 + (cc + 1) * 512], self.ones_col[:],
                             sqc[:, cc * 512:(cc + 1) * 512],
                             start=(dt == 0), stop=(dt == NDT - 1))
                s_pe.inc(t.last)
            mk[(tag + "stat", dt)] = s_pe.n

        self._rows_from_ps8(nc, mk, tag)

    def _rows_from_ps8(self, nc, mk, tag):
        block = self.block
        s_pe, s_dve, s_act = self.s_pe, self.s_dve, self.s_act
        PS8 = self.PS8
        rows = self.rows
        d_stat = s_pe.n

        @block.vector
        def _(v, d=d_stat):
            v.wait_ge(s_pe.h, d)
            for cc in range(4):
                v.tensor_scalar_mul(rows[0:1, cc * 512:(cc + 1) * 512],
                                    PS8[0:1, cc * 512:(cc + 1) * 512], 1.0 / D)
            for cc in range(4):
                v.tensor_scalar_mul(rows[0:1, 2048 + cc * 512: 2048 + (cc + 1) * 512],
                                    PS8[0:1, 2048 + cc * 512: 2048 + (cc + 1) * 512], 1.0 / D)
            s_dve.inc(v.engine_nop())
        d_r1 = s_dve.n

        @block.vector
        def _(v, d=d_r1):
            v.tensor_mul(PS8[0:1, 0:2048], rows[0:1, 0:2048], rows[0:1, 0:2048])
            v.tensor_sub(rows[0:1, 2048:4096], rows[0:1, 2048:4096], PS8[0:1, 0:2048])
            s_dve.inc(v.engine_nop())
        d_var = s_dve.n

        @block.scalar
        def _(s, d=d_var):
            s.wait_ge(s_dve.h, d)
            s.activation(PS8[0:1, 0:2048], rows[0:1, 2048:4096], AF.Ln, bias=EPS_LN)
            s_act.inc(s.activation(rows[0:1, 2048:4096], PS8[0:1, 0:2048],
                                   AF.Exp, scale=-0.5))
        d_rstd = s_act.n

        @block.vector
        def _(v, d=d_rstd):
            v.wait_ge(s_act.h, d)
            v.scalar_tensor_tensor(rows[0:1, 0:2048], rows[0:1, 0:2048], -1.0,
                                   rows[0:1, 2048:4096], op0=ALU.mult, op1=ALU.mult)
            s_dve.inc(v.engine_nop())
        mk[(tag + "rows",)] = s_dve.n

    def _rows_and_bcast(self, nc, bcA, bcB, mk, tag):
        self._bcast_from_rows(nc, bcA, bcB, mk, tag)

    def _bcast_from_rows(self, nc, bcA, bcB, mk, tag):
        block = self.block
        s_pe, s_dve = self.s_pe, self.s_dve
        PS8 = self.PS8
        rows = self.rows
        d_rows = mk[(tag + "rows",)]
        for cc in range(4):
            @block.tensor
            def _(t, cc=cc, d=d_rows):
                t = _PE(t)
                t.wait_ge(s_dve.h, d)
                if cc >= 1:
                    s_dve.wait(t, mk[(tag + "bcc", cc - 1)])
                t.matmul(PS8[:, 0:512], self.ones_row[:],
                         rows[0:1, 2048 + cc * 512: 2048 + (cc + 1) * 512],
                         start=True, stop=True)
                t.matmul(PS8[:, 512:1024], self.ones_row[:],
                         rows[0:1, cc * 512:(cc + 1) * 512],
                         start=True, stop=True)
                s_pe.inc(t.last)
            d_mm = s_pe.n

            @block.vector
            def _(v, cc=cc, d=d_mm):
                v.wait_ge(s_pe.h, d)
                v.tensor_copy(bcA[:, cc * 512:(cc + 1) * 512], PS8[:, 0:512])
                v.tensor_copy(bcB[:, cc * 512:(cc + 1) * 512], PS8[:, 512:1024])
                s_dve.inc(v.engine_nop())
            mk[(tag + "bcc", cc)] = s_dve.n

    def phase_ffn_a(self, nc):
        block = self.block
        s_sy, s_gp, s_pe, s_dve, s_act = self.s_sy, self.s_gp, self.s_pe, self.s_dve, self.s_act
        PS8 = self.PS8
        self.join()
        with ExitStack() as p5:
            sb5 = lambda name, shape, dt: p5.enter_context(nc.sbuf_tensor(name, list(shape), dt))
            wg_sb = [sb5(f"wg_sb{i}", [128, 2048], BF16) for i in range(2)]
            wu_sb = [sb5(f"wu_sb{i}", [128, 2048], BF16) for i in range(2)]
            silu_sb = [sb5(f"silu_sb{i}", [128, 1024], BF16) for i in range(2)]
            gated_sb = [sb5(f"gated_sb{i}", [128, 2048], BF16) for i in range(2)]
            mk = {}
            for fs in range(64):
                pw = fs % 2

                @block.gpsimd
                def _(g, fs=fs, pw=pw):
                    if fs >= 2:
                        s_pe.wait(g, mk[("gumm", fs - 2, 1)])
                    self.s_w[pw].inc(g.dma_start(
                        out=wg_sb[pw][:].rearrange("p (n c) -> p n c", n=NDT),
                        in_=self.wgT.rearrange("(n p) c -> p n c", p=128)
                            [:, :, fs * 128:(fs + 1) * 128]), 16)
                    self.s_w[pw].inc(g.dma_start(
                        out=wu_sb[pw][:].rearrange("p (n c) -> p n c", n=NDT),
                        in_=self.wuT.rearrange("(n p) c -> p n c", p=128)
                            [:, :, fs * 128:(fs + 1) * 128]), 16)
                d_w = self.s_w[pw].n

                for th in range(2):
                    st = (fs * 2 + th) % 2

                    @block.tensor
                    def _(t, d=d_w, fs=fs, th=th, st=st, pw=pw):
                        t = _PE(t)
                        t.wait_ge(self.s_w[pw].h, d)
                        k = (fs * 2 + th)
                        if k >= 2:
                            s_dve.wait(t, mk[("guev", k - 2)])
                        for dt in range(NDT):
                            for nn in range(2):
                                rhs = self.ln2T[:, dt * TOK + th * 1024 + nn * 512:
                                                dt * TOK + th * 1024 + (nn + 1) * 512]
                                t.matmul(PS8[:, st * 2048 + nn * 512: st * 2048 + (nn + 1) * 512],
                                         wg_sb[pw][:, dt * 128:(dt + 1) * 128], rhs,
                                         start=(dt == 0), stop=(dt == NDT - 1))
                                t.matmul(PS8[:, st * 2048 + 1024 + nn * 512:
                                             st * 2048 + 1024 + (nn + 1) * 512],
                                         wu_sb[pw][:, dt * 128:(dt + 1) * 128], rhs,
                                         start=(dt == 0), stop=(dt == NDT - 1))
                        s_pe.inc(t.last)
                    mk[("gumm", fs, th)] = d_mm = s_pe.n

                    @block.scalar
                    def _(s, d=d_mm, st=st, th=th, fs=fs):
                        s.wait_ge(s_pe.h, d)
                        if fs >= 1:
                            s_dve.wait(s, mk.get(("guev", (fs - 1) * 2 + th), 0))
                        s_act.inc(s.activation(silu_sb[th][:], PS8[:, st * 2048: st * 2048 + 1024],
                                               AF.Silu))
                    d_silu = s_act.n

                    @block.vector
                    def _(v, d=d_silu, fs=fs, th=th, st=st):
                        v.wait_ge(s_act.h, d)
                        if fs >= 2 and th == 0:
                            v.wait_ge(self.s_st[fs % 2].h, mk[("gdma", fs - 2)])
                        v.tensor_mul(gated_sb[fs % 2][:, th * 1024:(th + 1) * 1024],
                                     PS8[:, st * 2048 + 1024: st * 2048 + 2048], silu_sb[th][:])
                        s_dve.inc(v.engine_nop())
                    mk[("guev", fs * 2 + th)] = s_dve.n
                d_gu = s_dve.n

                @block.sync
                def _(sy, d=d_gu, fs=fs):
                    sy.wait_ge(s_dve.h, d)
                    self.s_st[fs % 2].inc(sy.dma_start(
                        out=self.gated_sc.rearrange("(n p) q -> p n q", p=128)[:, fs, :],
                        in_=gated_sb[fs % 2][:]), 16)
                mk[("gdma", fs)] = self.s_st[fs % 2].n

            t_g = self.tap("gated", [FF, TOK], BF16)
            if t_g is not None:
                @block.sync
                def _(sy):
                    for s in self.s_st:
                        s.wait(sy)
                    s_sy.inc(sy.dma_start(out=t_g, in_=self.gated_sc), 16)
                    sy.wait_ge(s_sy.h, s_sy.n)

    def phase_down(self, nc):
        block = self.block
        s_sy, s_gp, s_pe, s_dve, s_act = self.s_sy, self.s_gp, self.s_pe, self.s_dve, self.s_act
        PS8 = self.PS8
        self.join()
        with ExitStack() as p6:
            sb6 = lambda name, shape, dt: p6.enter_context(nc.sbuf_tensor(name, list(shape), dt))
            wd_sb = sb6("wd_sb", [128, 64 * 1024], BF16)
            gch = [sb6(f"gch{i}", [128, 16 * 512], BF16) for i in range(2)]
            dn_ev = [sb6("dn_ev0", [128, 4096], F32)] * 2
            mk = {}
            for dg in range(2):
                for piece in range(4):
                    @block.gpsimd
                    def _(g, dg=dg, piece=piece):
                        if dg == 1:
                            s_pe.wait(g, mk[("dmm", 0, 3, 3)])
                        self.s_wd[piece].inc(g.dma_start(
                            out=wd_sb[:, piece * 16 * 1024:(piece + 1) * 16 * 1024]
                                .rearrange("p (n c) -> p n c", n=16),
                            in_=self.wdT.rearrange("(n p) c -> p n c", p=128)
                                [:, piece * 16:(piece + 1) * 16, dg * 1024:(dg + 1) * 1024]), 16)
                    mk[("wdp", dg, piece)] = self.s_wd[piece].n

                for tq in range(4):
                    for fsg in range(4):
                        pg = (tq * 4 + fsg) % 2

                        @block.sync
                        def _(sy, dg=dg, tq=tq, fsg=fsg, pg=pg):
                            s_pe.wait(sy, mk.get(("gchp", pg), 0))
                            self.s_ld[pg].inc(sy.dma_start(
                                out=gch[pg][:].rearrange("p (n c) -> p n c", n=16),
                                in_=self.gated_sc.rearrange("(n p) q -> p n q", p=128)
                                    [:, fsg * 16:(fsg + 1) * 16, tq * 512:(tq + 1) * 512]), 16)
                        d_g = self.s_ld[pg].n

                        @block.tensor
                        def _(t, d=d_g, dg=dg, tq=tq, fsg=fsg, pg=pg):
                            t = _PE(t)
                            t.wait_ge(self.s_ld[pg].h, d)
                            t.wait_ge(self.s_wd[fsg].h, mk[("wdp", dg, fsg)])
                            if fsg == 0 and (dg, tq) != (0, 0):
                                s_dve.wait(t, mk.get(("dnev", dg, tq - 1), mk.get(("dnev", dg - 1, 3), 0)))
                            for fs in range(16):
                                for ds in range(8):
                                    t.matmul(PS8[:, ds * 512:(ds + 1) * 512],
                                             wd_sb[:, (fsg * 16 + fs) * 1024 + ds * 128:
                                                   (fsg * 16 + fs) * 1024 + ds * 128 + 128],
                                             gch[pg][:, fs * 512:(fs + 1) * 512],
                                             start=(fsg == 0 and fs == 0),
                                             stop=(fsg == 3 and fs == 15))
                            s_pe.inc(t.last)
                        mk[("dmm", dg, tq, fsg)] = mk[("gchp", pg)] = s_pe.n
                    d_dmm = s_pe.n

                    @block.vector
                    def _(v, d=d_dmm, dg=dg, tq=tq):
                        v.wait_ge(s_pe.h, d)
                        v.wait_ge(self.s_st[0].h, mk.get(("dndmap", 0), 0))
                        v.wait_ge(self.s_st[1].h, mk.get(("dndmap", 1), 0))
                        v.tensor_copy(dn_ev[tq % 2][:], PS8[:, 0:4096])
                        s_dve.inc(v.engine_nop())
                    mk[("dnevlast",)] = d_ev = s_dve.n

                    @block.sync
                    def _(sy, d=d_ev, dg=dg, tq=tq):
                        sy.wait_ge(s_dve.h, d)
                        self.s_st[tq % 2].inc(sy.dma_start(
                            out=self.down_sc.rearrange("(a p) t -> p a t", p=128)
                                [:, dg * 8:(dg + 1) * 8, tq * 512:(tq + 1) * 512],
                            in_=dn_ev[tq % 2][:].rearrange("p (a t) -> p a t", a=8)), 16)
                    mk[("dndmap", tq % 2)] = self.s_st[tq % 2].n

    def phase_epilogue(self, nc):
        block = self.block
        s_sy, s_gp, s_pe, s_dve, s_act = self.s_sy, self.s_gp, self.s_pe, self.s_dve, self.s_act
        PS8 = self.PS8
        rows = self.rows
        self.join()
        with ExitStack() as p7:
            sb7 = lambda name, shape, dt: p7.enter_context(nc.sbuf_tensor(name, list(shape), dt))
            dn_rd = [sb7(f"dn_rd{i}", [128, 8192], F32) for i in range(2)]
            x1_rd = [sb7(f"x1_rd{i}", [128, 8192], F32) for i in range(2)]
            bfc = sb7("bfc7", [128, 8192], BF16)
            sqc = sb7("sqc7", [128, 8192], BF16)
            tE = sb7("tE", [128, 512], F32)
            mk = {}
            for tq in range(4):
                pq = tq % 2

                @block.sync
                def _(sy, tq=tq, pq=pq):
                    if tq >= 2:
                        s_dve.wait(sy, mk[("outdone", tq - 2)])
                    self.s_ld[pq].inc(sy.dma_start(
                        out=dn_rd[pq][:].rearrange("p (a t) -> p a t", a=NDT),
                        in_=self.down_sc.rearrange("(a p) t -> p a t", p=128)
                            [:, :, tq * 512:(tq + 1) * 512]), 16)
                    self.s_ld[pq].inc(sy.dma_start(
                        out=x1_rd[pq][:].rearrange("p (a t) -> p a t", a=NDT),
                        in_=self.x1_sc.rearrange("(a p) t -> p a t", p=128)
                            [:, :, tq * 512:(tq + 1) * 512]), 16)
                d_ld = self.s_ld[pq].n

                @block.vector
                def _(v, d=d_ld, tq=tq, pq=pq):
                    v.wait_ge(self.s_ld[pq].h, d)
                    if tq >= 1:
                        s_pe.wait(v, mk[("estat", tq - 1)])
                    v.tensor_copy(bfc[:], dn_rd[pq][:])
                    s_dve.inc(v.engine_nop())
                d_bf = s_dve.n

                @block.scalar
                def _(s, d=d_ld, tq=tq, pq=pq):
                    s.wait_ge(self.s_ld[pq].h, d)
                    if tq >= 1:
                        s_pe.wait(s, mk[("estat", tq - 1)])
                    s_act.inc(s.activation(sqc[:], dn_rd[pq][:], AF.Square))
                d_sq = s_act.n

                @block.tensor
                def _(t, d1=d_bf, d2=d_sq, tq=tq):
                    t = _PE(t)
                    t.wait_ge(s_dve.h, d1)
                    t.wait_ge(s_act.h, d2)
                    if tq >= 1:
                        s_dve.wait(t, mk[("erows", tq - 1)])
                    for dt in range(NDT):
                        t.matmul(PS8[0:1, 0:512], self.ones_col[:],
                                 bfc[:, dt * 512:(dt + 1) * 512],
                                 start=(dt == 0), stop=(dt == NDT - 1))
                        t.matmul(PS8[0:1, 512:1024], self.ones_col[:],
                                 sqc[:, dt * 512:(dt + 1) * 512],
                                 start=(dt == 0), stop=(dt == NDT - 1))
                    s_pe.inc(t.last)
                mk[("estat", tq)] = d_stat = s_pe.n

                @block.vector
                def _(v, d=d_stat):
                    v.wait_ge(s_pe.h, d)
                    v.tensor_scalar_mul(rows[0:1, 0:512], PS8[0:1, 0:512], 1.0 / D)
                    v.tensor_scalar_mul(rows[0:1, 512:1024], PS8[0:1, 512:1024], 1.0 / D)
                    v.tensor_mul(rows[0:1, 1024:1536], rows[0:1, 0:512], rows[0:1, 0:512])
                    v.tensor_sub(rows[0:1, 1024:1536], rows[0:1, 512:1024], rows[0:1, 1024:1536])
                    s_dve.inc(v.engine_nop())
                d_var = s_dve.n

                @block.scalar
                def _(s, d=d_var):
                    s.wait_ge(s_dve.h, d)
                    s.activation(rows[0:1, 1536:2048], rows[0:1, 1024:1536], AF.Ln, bias=EPS_LN)
                    s_act.inc(s.activation(rows[0:1, 2048:2560], rows[0:1, 1536:2048],
                                           AF.Exp, scale=-0.5))
                d_rstd = s_act.n

                @block.vector
                def _(v, d=d_rstd):
                    v.wait_ge(s_act.h, d)
                    v.scalar_tensor_tensor(rows[0:1, 2560:3072], rows[0:1, 0:512], -1.0,
                                           rows[0:1, 2048:2560], op0=ALU.mult, op1=ALU.mult)
                    s_dve.inc(v.engine_nop())
                mk[("erows", tq)] = d_rows = s_dve.n

                @block.tensor
                def _(t, d=d_rows, tq=tq):
                    t = _PE(t)
                    t.wait_ge(s_dve.h, d)
                    if tq >= 1:
                        s_dve.wait(t, mk[("ebc", tq - 1)])
                    t.matmul(PS8[:, 1024:1536], self.ones_row[:], rows[0:1, 2048:2560],
                             start=True, stop=True)
                    t.matmul(PS8[:, 1536:2048], self.ones_row[:], rows[0:1, 2560:3072],
                             start=True, stop=True)
                    s_pe.inc(t.last)
                d_bc = s_pe.n

                @block.vector
                def _(v, d=d_bc, tq=tq, pq=pq):
                    v.wait_ge(s_pe.h, d)
                    for dt in range(NDT):
                        cs = slice(dt * 512, (dt + 1) * 512)
                        v.tensor_mul(tE[:], dn_rd[pq][:, cs], PS8[:, 1024:1536])
                        v.tensor_add(tE[:], tE[:], PS8[:, 1536:2048])
                        v.tensor_add(x1_rd[pq][:, cs], tE[:], x1_rd[pq][:, cs])
                    s_dve.inc(v.engine_nop())
                mk[("ebc", tq)] = mk[("outdone", tq)] = d_out = s_dve.n

                @block.sync
                def _(sy, d=d_out, tq=tq, pq=pq):
                    sy.wait_ge(s_dve.h, d)
                    s_sy.inc(sy.dma_start(
                        out=self.outT.rearrange("(a p) t -> p a t", p=128)
                            [:, :, tq * 512:(tq + 1) * 512],
                        in_=x1_rd[pq][:].rearrange("p (a t) -> p a t", a=NDT)), 16)





def own_indices(side):
    return np.concatenate([np.arange((2 * j + side) * 128, (2 * j + side) * 128 + 128)
                           for j in range(NT)])


def rope_tables():
    i = np.arange(128, dtype=np.float64)
    inv = 1.0 / (1e6 ** (2.0 * i / 256.0))
    pos = np.arange(L, dtype=np.float64)
    fr = inv[:, None] * pos[None, :]
    return np.cos(fr).astype(np.float32), np.sin(fr).astype(np.float32)


def make_masks(side):
    m = np.zeros((NT, 128, 256), np.float32)
    kp = np.arange(128)
    qi = np.arange(128)
    for j in range(NT):
        qg = (2 * j + side) * 128 + qi
        for i in range(2):
            kg = (2 * j + i) * 128 + kp
            m[j, :, i * 128:(i + 1) * 128] = np.where(
                kg[None, :] <= qg[:, None], 1.0, 0.0).astype(np.float32)
    return m.astype(ml_dtypes.bfloat16)


def core_inputs(inp, core):
    b, s = core // 2, core % 2
    x = np.asarray(inp["L0_attn_x"])
    xT = np.ascontiguousarray(x[b].T)
    oi = own_indices(s)
    cosT, sinT = rope_tables()
    T = lambda a: np.ascontiguousarray(np.asarray(a).T)
    qn = np.asarray(inp["L0_attn_q_norm"]); kn = np.asarray(inp["L0_attn_k_norm"])
    d = {
        "xT_full": xT,
        "xT_own": np.ascontiguousarray(xT[:, oi]),
        "wqT": T(inp["L0_attn_q_proj"]),
        "wkT": T(inp["L0_attn_k_proj"]),
        "wvT": T(inp["L0_attn_v_proj"]),
        "woT": T(inp["L0_attn_o_proj"]),
        "wgT": T(inp["L0_ffn_w_gate"]),
        "wuT": T(inp["L0_ffn_w_up"]),
        "wdT": np.ascontiguousarray(np.asarray(inp["L0_ffn_w_down"]).T),
        "qkn": np.ascontiguousarray(np.stack([qn[:128], qn[128:], kn[:128], kn[128:]], axis=1)).astype(np.float32),
        "cosT": cosT, "sinT": sinT,
        "cos_own": np.ascontiguousarray(cosT[:, oi]),
        "sin_own": np.ascontiguousarray(sinT[:, oi]),
        "masks": make_masks(s),
        "id_f32": np.eye(128, dtype=np.float32),
    }
    return {k: (v.astype(np.float32) if v.dtype == np.float64 else v) for k, v in d.items()}


def assemble_out(results):
    out = np.empty((B, L, D), np.float32)
    for core, r in enumerate(results):
        b, s = core // 2, core % 2
        oi = own_indices(s)
        out[b, oi, :] = r["outT"].T
    return out


_CACHE = {}


def kernel(**inputs):
    key = "nc"
    if key not in _CACHE:
        m = MK(phases=9)
        _CACHE[key] = m.build()
    nc = _CACHE[key]
    in_maps = [core_inputs(inputs, c) for c in range(8)]
    res = run_bass_kernel_spmd(nc, in_maps, core_ids=list(range(8)))
    return assemble_out(res.results)
